# version 20
# baseline (speedup 1.0000x reference)
import contextlib

import numpy as np
import ml_dtypes

import concourse.bass as bass
import concourse.mybir as mybir
from concourse.bass_utils import run_bass_kernel_spmd

F32 = mybir.dt.float32
BF16 = mybir.dt.bfloat16
AF = mybir.ActivationFunctionType
ALU = mybir.AluOpType
AX = mybir.AxisListType

S, D, E, FF, DS = 2048, 1024, 2048, 4096, 16
NCH, NB, BLK, NT, ND, NE, NF = 16, 4, 512, 16, 8, 16, 32
EPS = 1e-5

SL_WIN0, SL_WOUT0, SL_FFN0 = 0, 16, 32
NSLICE = 96


def _ffn_slice(idx):
    j = idx - SL_FFN0
    q, r = divmod(j, 16)
    return (q, "w1", r) if r < 8 else (q, "w2", r - 8)


def _ffn_gate_pc(idx):
    q, kind, i = _ffn_slice(idx)
    if q >= 2:
        return None
    qp = q % 2
    col = 16384 * qp + (1024 * i if kind == "w1" else 8192 + 1024 * i)
    return col // 2048 + 1


def _mk_evt(names_):
    return {nm: i + 1 for i, nm in enumerate(names_)}


EVT_PTB = _mk_evt(
    ["atT"] + [f"dbl{i}" for i in range(8)] + ["grow", "bvt", "bmatJ"]
    + ["pdA", "pdB", "pdC", "pdD", "pdR", "gt", "qt0", "qt1"])
EVT_LAD = _mk_evt(["init"] + [f"dbl{i}" for i in range(8)])
EVT_TBE = _mk_evt(["pd1c", "bmat", "bquad", "pdAc", "pdBc", "pdCc", "pdDc",
                   "gtbf", "qtbf0", "qtbf1"])
EVT_TBL = _mk_evt(["brw", "qtfz", "gext", "gtrp"]
                  + [f"pdr{d}" for d in range(15)] + ["qtrp0", "qtrp1"])
TBL16 = {k: 16 * v for k, v in EVT_TBL.items()}

PDPAIRS = {2: (1, 1), 3: (1, 2), 4: (2, 2), 5: (1, 4), 6: (2, 4), 7: (3, 4),
           8: (4, 4), 9: (1, 8), 10: (2, 8), 11: (3, 8), 12: (4, 8),
           13: (5, 8), 14: (6, 8)}
PDRND = [("pdA", "pd1c", [2]), ("pdB", "pdAc", [3, 4]),
         ("pdC", "pdBc", [5, 6, 7, 8]), ("pdD", "pdCc", [9, 10, 11, 12, 13, 14])]


NCH_C = [NCH]
C25_PARTS = [7]


def build_program(debug=False, phase_limit=4, rd_drains=False, TAIL_ON_ACT=True):
    nc = bass.Bass("TRN2", target_bir_lowering=False)

    x_h = nc.dram_tensor("x", [S, D], F32, kind="ExternalInput").ap()
    Win_h = nc.dram_tensor("W_in", [D, E], F32, kind="ExternalInput").ap()
    Wout_h = nc.dram_tensor("W_out", [E, D], F32, kind="ExternalInput").ap()
    W1_h = nc.dram_tensor("W1", [D, FF], F32, kind="ExternalInput").ap()
    W2_h = nc.dram_tensor("W2", [FF, D], F32, kind="ExternalInput").ap()
    A_h = nc.dram_tensor("A", [DS, DS], F32, kind="ExternalInput").ap()
    Bp_h = nc.dram_tensor("Bp", [DS, 1], F32, kind="ExternalInput").ap()
    Cp_h = nc.dram_tensor("Cp", [DS, 1], F32, kind="ExternalInput").ap()
    Dp_h = nc.dram_tensor("Dp", [1, 1], F32, kind="ExternalInput").ap()
    binc_h = nc.dram_tensor("b_in_cm", [128, NE], F32, kind="ExternalInput").ap()
    cwc_h = nc.dram_tensor("conv_w_cm", [128, 4 * NE], F32, kind="ExternalInput").ap()
    cbc_h = nc.dram_tensor("conv_b_cm", [128, NE], F32, kind="ExternalInput").ap()
    b1c_h = nc.dram_tensor("b1_cm", [128, NF], F32, kind="ExternalInput").ap()
    brows_h = nc.dram_tensor("brows", [2, D], F32, kind="ExternalInput").ap()
    i128_h = nc.dram_tensor("i128bf", [128, 128], BF16, kind="ExternalInput").ap()
    j128_h = nc.dram_tensor("j128f", [128, 128], F32, kind="ExternalInput").ap()
    j16_h = nc.dram_tensor("j16f", [DS, DS], F32, kind="ExternalInput").ap()
    i16_h = nc.dram_tensor("i16f", [DS, DS], F32, kind="ExternalInput").ap()
    ones_h = nc.dram_tensor("ones2x128", [2, 128], BF16, kind="ExternalInput").ap()

    out_h = nc.dram_tensor("out", [S, D], F32, kind="ExternalOutput").ap()

    gext_h = nc.dram_tensor("g_ext", [1, 256], F32, kind="Internal").ap()
    qtf_h = nc.dram_tensor("qtF", [1, 8192], F32, kind="Internal").ap()
    oacc_h = nc.dram_tensor("oacc", [S, D], F32, kind="Internal").ap()

    dbg = {}
    DBG_TAPS = (("xnT", [128, ND * S], BF16), ("gtbf", [128, 128], BF16),
                ("ctbf", [DS, 128], BF16), ("ct32", [128, 128], BF16),
                ("qtbf", [128, 512], BF16), ("bquad", [128, 256], BF16),
                ("rt", [DS, 256], F32), ("bv", [DS, 256], F32),
                ("AR3", [128, 16384], BF16), ("sall", [128, 2 * E], BF16),
                ("AR2", [128, NCH * E], BF16),
                ("cbp", [128, NE], F32), ("brow", [2, D], BF16))
    if debug:
        for nm, shape, dt in DBG_TAPS:
            dbg[nm] = nc.dram_tensor("dbg_" + nm, shape, dt, kind="ExternalOutput").ap()
        dbg["oacc1"] = nc.dram_tensor("dbg_oacc1", [S, D], F32, kind="ExternalOutput").ap()

    ctx = contextlib.ExitStack()
    sbuf = lambda name, shape, dtype: ctx.enter_context(nc.sbuf_tensor(name, shape, dtype))
    sem = lambda name: ctx.enter_context(nc.semaphore(name))

    i128 = sbuf("i128", [128, 128], BF16)
    j128 = sbuf("j128", [128, 128], F32)
    j16 = sbuf("j16", [DS, DS], F32)
    i16 = sbuf("i16", [DS, DS], F32)
    ones1 = sbuf("ones1", [2, 128], BF16)
    a_sb = sbuf("a_sb", [DS, DS], F32)
    bp_sb = sbuf("bp_sb", [DS, 1], F32)
    cp_sb = sbuf("cp_sb", [DS, 1], F32)
    dp_sb = sbuf("dp_sb", [1, 1], F32)
    binc = sbuf("binc", [128, NE], F32)
    cwc = sbuf("cwc", [128, 4 * NE], F32)
    cbc = sbuf("cbc", [128, NE], F32)
    b1c = sbuf("b1c", [128, NF], F32)
    brow = sbuf("brow", [2, D], BF16)

    at2 = sbuf("at2", [DS, 2 * DS], F32)
    powk = sbuf("powk", [DS, 2 * DS], F32)
    pdL = sbuf("pdL", [DS, 8 * DS], F32)
    pdH = sbuf("pdH", [DS, 14 * DS], F32)
    pdall = sbuf("pdall", [DS, 14 * DS], F32)
    rt = sbuf("rt", [DS, 256], F32)
    bv = sbuf("bv", [DS, 256], F32)
    grow = sbuf("grow", [1, 256], F32)
    zq = sbuf("zq", [128, 64], F32)
    gtrp = sbuf("gtrp", [128, 128], F32)
    gtbf = sbuf("gtbf", [128, 128], BF16)
    ctbf = sbuf("ctbf", [DS, 128], BF16)
    ct32 = sbuf("ct32", [128, 128], BF16)
    bmat_s = sbuf("bmat_s", [128, DS], F32)
    bquad = sbuf("bquad", [128, 256], BF16)
    qtrp = sbuf("qtrp", [128, 512], F32)
    qtbf = sbuf("qtbf", [128, 512], BF16)
    cbp = sbuf("cbp", [128, NE], F32)
    wsum = sbuf("wsum", [128, NE], F32)

    stat6 = sbuf("stat6", [128, 12], F32)
    mv = sbuf("mv", [128, 8], F32)
    invv = sbuf("invv", [128, 4], F32)
    rstd = sbuf("rstd", [128, 4], F32)
    negm = sbuf("negm", [128, 4], F32)
    negp = sbuf("negp", [128, 4], F32)
    xnb = sbuf("xnb", [128, 2 * D], BF16)
    xnT_t = sbuf("xnT", [128, ND * S], BF16)
    uhalo = sbuf("uhalo", [128, 2 * 3 * NE], F32)
    sall = sbuf("sall", [128, 2 * E], BF16)

    AR1 = sbuf("AR1", [128, 16384], BF16)
    AR2 = sbuf("AR2", [128, NCH * E], BF16)
    AR3 = sbuf("AR3", [128, 16384], BF16)
    AR4 = sbuf("AR4", [128, 2 * D], F32)
    AR5 = sbuf("AR5", [128, 3078], F32)

    winb = lambda k: AR1[:, E * k : E * (k + 1)]
    woutb = lambda s: AR1[:, D * s : D * (s + 1)]
    useq = lambda c: AR2[:, E * c : E * (c + 1)]
    w1q = lambda qp, k: AR2[:, 16384 * qp + 1024 * k : 16384 * qp + 1024 * (k + 1)]
    w2q = lambda qp, f: AR2[:, 16384 * qp + 8192 + 1024 * f : 16384 * qp + 8192 + 1024 * (f + 1)]
    ucv = lambda par, s_: AR3[:, 8192 * par + BLK * s_ : 8192 * par + BLK * (s_ + 1)]
    xnb_t = lambda t: xnb[:, D * (t % 2) : D * (t % 2 + 1)]
    hstage = lambda t: AR3[:, E * t : E * (t + 1)]
    hall3 = lambda t: AR3[:, 4096 + E * t : 4096 + E * (t + 1)]
    hTf = lambda par, f: AR3[:, 4096 * par + BLK * f : 4096 * par + BLK * (f + 1)]
    xin = lambda p: AR4[:, D * p : D * (p + 1)]
    xr = xin
    oar = xin
    utile = lambda p: AR5[:, 515 * p : 515 * (p + 1)]
    acc3b = lambda p: AR5[:, 1030 + 512 * p : 1030 + 512 * (p + 1)]
    wkb = lambda p: AR5[:, 2054 + 512 * p : 2054 + 512 * (p + 1)]
    oat = lambda p: AR5[:, D * p : D * (p + 1)]
    outt = oat
    ytc = lambda c: sall[:, E * (c % 2) : E * (c % 2 + 1)]
    pl = lambda d: pdL[:, DS * (d - 1) : DS * d]
    ph = lambda d: pdH[:, DS * (d - 1) : DS * d]
    ar2f = AR2.bitcast(F32)
    xview = lambda t: ar2f[:, D * (t - 2) : D * (t - 1)] if t in (2, 3) else xin(t % 2)

    ps_arena_h = ctx.enter_context(nc.psum_tensor("ps_arena", [128, 4096], F32))
    ps_bc_h = ps_arena_h.bitcast(BF16)

    def bankf(b, n):
        return ps_arena_h[:, 512 * b : 512 * b + 512 * n]

    def bankb(b, n):
        return ps_bc_h[:, 1024 * b : 1024 * b + 1024 * n]

    pm_lad = bankf(0, 1)
    pm_qt = bankf(6, 1)[:, 0:256]
    pw = [bankf(3, 1), bankf(4, 1), bankf(2, 1)]
    pxt2 = [bankb(5, 1), bankb(2, 1)]
    putA = bankb(6, 2)
    putB = [bankb(5, 1), bankb(2, 1)]
    psq2 = [ps_arena_h[0:64, 0:1024], ps_arena_h[64:128, 0:1024]]
    pb0 = bankf(3, 2)
    pb1 = bankf(5, 2)
    pc = bankf(0, 4)
    po1 = [bankf(4, 2), bankf(6, 2)]
    phh = [bankf(i, 1) for i in range(4)]
    po2 = [bankf(4, 2), bankf(6, 2)]

    names_ = ("s_small s_wdma s_rsd s_ln s_lnS s_ptx s_xnT s_pwin s_usb s_ucv "
              "s_ptu s_useq s_pA s_sall s_tbl s_lad s_gex s_pdt s_ptb s_tbe s_pB "
              "s_hst s_hrl s_pC s_yt s_pO1 s_oat s_ph s_hT s_pO2 "
              "s_o2t s_out s_tap s_mz s_cbp s_ct3 s_sm2").split()
    SM = {n: sem(n) for n in names_}
    (s_small, s_wdma, s_rsd, s_ln, s_lnS, s_ptx, s_xnT, s_pwin, s_usb, s_ucv,
     s_ptu, s_useq, s_pA, s_sall, s_tbl, s_lad, s_gex, s_pdt, s_ptb, s_tbe, s_pB,
     s_hst, s_hrl, s_pC, s_yt, s_pO1, s_oat, s_ph, s_hT, s_pO2,
     s_o2t, s_out, s_tap, s_mz, s_cbp, s_ct3, s_sm2) = (SM[n] for n in names_)
    s_wq = [sem(f"s_wq{i}") for i in range(4)]
    s_x2 = [sem("s_x0"), sem("s_x1")]
    s_x23 = [sem("s_x23a"), sem("s_x23b")]
    s_xr2 = [sem("s_xr0"), sem("s_xr1")]
    s_oar2 = [sem("s_oar0"), sem("s_oar1")]
    st_oacc2 = [sem("st_oacc0"), sem("st_oacc1")]
    st_add2 = [sem("st_add0"), sem("st_add1")]
    st_out2 = [sem("st_out0"), sem("st_out1")]

    NSMALL = 13

    blkc = ctx.enter_context(nc.Block())

    _dbg_done = [0]

    def emit_dbg_taps(gpsimd):
        sbt = {"xnT": xnT_t, "gtbf": gtbf, "ctbf": ctbf, "ct32": ct32,
               "qtbf": qtbf, "bquad": bquad, "rt": rt, "bv": bv,
               "AR3": AR3, "sall": sall, "cbp": cbp, "brow": brow,
               "AR2": AR2}
        nd = 0
        for nm, shape, dt in DBG_TAPS:
            if nd:
                gpsimd.wait_ge(s_tap, 16 * nd)
            gpsimd.dma_start(out=dbg[nm], in_=sbt[nm][:]).then_inc(s_tap, 16)
            nd += 1
        if phase_limit in (3, 4):
            gpsimd.wait_ge(s_tap, 16 * nd)
            gpsimd.dma_start(out=dbg["oacc1"], in_=oacc_h).then_inc(s_tap, 16)
            nd += 1
        gpsimd.wait_ge(s_tap, 16 * nd)

    CONSTS = [(i128[:], i128_h), (a_sb[:], A_h), (i16[:], i16_h),
              (bp_sb[:], Bp_h), (cp_sb[:], Cp_h), (dp_sb[:], Dp_h),
              (binc[:], binc_h), (cwc[:], cwc_h), (cbc[:], cbc_h),
              (j16[:], j16_h), (j128[:], j128_h), (b1c[:], b1c_h),
              (ones1[:], ones_h)]
    SM6, SM7 = 16 * 6, 16 * 7

    @blkc.sync
    def _(sync):
        sync.dma_start(out=xin(0), in_=x_h[0:128, :]).then_inc(s_x2[0], 16)
        for dst, src in CONSTS[0:6]:
            sync.dma_start(out=dst, in_=src).then_inc(s_small, 16)
        for t in range(1, 4):
            sync.dma_start(out=xview(t), in_=x_h[128 * t : 128 * (t + 1), :]
                           ).then_inc(s_x2[1] if t == 1 else s_x23[t - 2], 16)
        for dst, src in CONSTS[6:]:
            sync.dma_start(out=dst, in_=src).then_inc(s_sm2, 16)
        for t in range(4, NT):
            sync.wait_ge(s_ln, t - 3 if t < 6 else t - 1)
            sync.dma_start(out=xin(t % 2), in_=x_h[128 * t : 128 * (t + 1), :]).then_inc(s_x2[t % 2], 16)
        sync.wait_ge(s_gex, 4 if TAIL_ON_ACT else 3)
        for q in range(3):
            sync.dma_start(out=ct32[32 * q : 32 * q + 16, :], in_=ctbf[:]
                           ).then_inc(s_ct3, 16)
        if phase_limit < 2:
            return
        for c in range(NCH):
            t3, m3 = divmod(c, 3)
            sync.wait_ge(s_mz, 1 if c < 6 else 2)
            sync.wait_ge(s_hst, c // 8 + 1)
            if c:
                sync.wait_ge(s_hrl, 16 * c)
            sync.dma_start(out=hall3(t3)[32 * m3 : 32 * m3 + 16, :],
                           in_=hstage(c // 8)[16 * (c % 8) : 16 * (c % 8) + 16, :]
                           ).then_inc(s_hrl, 16)
        if phase_limit < 3 or phase_limit == 25:
            return
        for c in range(NCH + 2):
            if c < NCH:
                if c >= 2:
                    sync.wait_ge(s_oat, c - 1)
                sync.dma_start(out=xr(c % 2),
                               in_=x_h[128 * c : 128 * (c + 1), :]).then_inc(s_xr2[c % 2], 16)
            if c >= 2:
                cc = c - 2
                sync.wait_ge(s_oat, cc + 1)
                sync.dma_start(out=oacc_h[128 * cc : 128 * (cc + 1), :],
                               in_=oat(cc % 2)).then_inc(st_oacc2[cc % 2], 16)
        if phase_limit < 4:
            return
        for tt in range(NT):
            sync.wait_ge(st_add2[tt % 2], 16 * (17 + tt // 2))
            if tt >= 2:
                sync.wait_ge(s_out, tt - 1)
            sync.dma_start(out=oar(tt % 2),
                           in_=oacc_h[128 * tt : 128 * (tt + 1), :]).then_inc(s_oar2[tt % 2], 16)

    @blkc.gpsimd
    def _(gpsimd):
        for k in range(ND):
            gpsimd.dma_start(out=winb(k), in_=Win_h[128 * k : 128 * (k + 1), :]
                             ).then_inc(s_wdma, 16)
        tblc = [0]

        def tdma(**kw):
            if tblc[0]:
                gpsimd.wait_ge(s_tbl, 16 * tblc[0])
            gpsimd.dma_start(**kw).then_inc(s_tbl, 16)
            tblc[0] += 1

        tdma(out=brow[:], in_=brows_h)
        gpsimd.wait_ge(s_gex, 1)
        tdma(out=bass.AP(tensor=qtf_h.tensor, offset=0, ap=[[64, 128], [1, 64]]),
             in_=zq[:])
        gpsimd.wait_ge(s_gex, 4 if TAIL_ON_ACT else 3)
        tdma(out=gext_h, in_=grow[:])
        tdma(out=gtrp[:],
             in_=bass.AP(tensor=gext_h.tensor, offset=1, ap=[[1, 128], [1, 128]]))
        gpsimd.wait_ge(s_sm2, SM7)
        gpsimd.wait_ge(s_tbl, 16 * tblc[0])
        gpsimd.wait_ge(s_pdt, 1)
        for d_ in range(15):
            src = j16[:] if d_ == 0 else pdall[:, DS * (d_ - 1) : DS * d_]
            gpsimd.dma_start(out=bass.AP(tensor=qtf_h.tensor, offset=(16 + d_) * 256,
                                         ap=[[16, 16], [1, 16]]),
                             in_=src).then_inc(s_tbl, 16)
            tblc[0] += 1
        for tile_i, off in ((0, 8 * 256), (1, 0)):
            src = bass.AP(tensor=qtf_h.tensor, offset=off,
                          ap=[[256, 8], [16, 16], [256, 16], [1, 16]])
            tdma(out=qtrp[:, 256 * tile_i : 256 * (tile_i + 1)], in_=src)
        if phase_limit < 3 or phase_limit == 25:
            gpsimd.wait_ge(s_sall, 2 * NB if phase_limit == 1 else 0)
            if phase_limit in (2, 25):
                gpsimd.wait_ge(s_hrl, 16 * NCH)
            if phase_limit == 25:
                gpsimd.wait_ge(s_yt, NCH_C[0])
            if debug:
                emit_dbg_taps(gpsimd)
            return
        gpsimd.wait_ge(s_pwin, NB * NE)
        for s_ in range(NE):
            gpsimd.dma_start(out=woutb(s_), in_=Wout_h[128 * s_ : 128 * (s_ + 1), :]
                             ).then_inc(s_wdma, 16)
        def q_loads(q):
            if q == 0:
                gpsimd.wait_ge(s_pO1, 1)
            for k in range(ND):
                gate = _ffn_gate_pc(SL_FFN0 + 16 * q + k)
                if gate is not None:
                    gpsimd.wait_ge(s_pC, gate)
                else:
                    gpsimd.wait_ge(s_ph, 32 * (q - 1))
                gpsimd.dma_start(out=w1q(q % 2, k),
                                 in_=W1_h[128 * k : 128 * (k + 1), 1024 * q : 1024 * (q + 1)]
                                 ).then_inc(s_wq[q], 16)
            for f in range(8):
                gate = _ffn_gate_pc(SL_FFN0 + 16 * q + 8 + f)
                if gate is not None:
                    gpsimd.wait_ge(s_pC, gate)
                elif phase_limit == 35:
                    gpsimd.wait_ge(s_ph, 32 * (q - 1))
                else:
                    gpsimd.wait_ge(s_pO2, 16 * (q - 1))
                gpsimd.dma_start(out=w2q(q % 2, f),
                                 in_=W2_h[128 * (8 * q + f) : 128 * (8 * q + f + 1), :]
                                 ).then_inc(s_wq[q], 16)

        def q_adds(qq):
            for tt in range(NT):
                qtt = 16 * qq + tt
                gpsimd.wait_ge(s_o2t, qtt + 1)
                gpsimd.wait_ge(st_oacc2[tt % 2], 16 * (tt // 2 + 1))
                gpsimd.dma_start(out=oacc_h[128 * tt : 128 * (tt + 1), :],
                                 in_=outt(qtt % 2), accum_op=ALU.add
                                 ).then_inc(st_add2[qtt % 2], 16)

        if phase_limit == 35:
            for q in range(4):
                q_loads(q)
            gpsimd.wait_ge(s_hT, 129)
        if phase_limit >= 4:
            q_loads(0)
            q_loads(1)
            q_adds(0)
            q_loads(2)
            q_adds(1)
            q_loads(3)
            q_adds(2)
        if phase_limit >= 4:
            for tt in range(NT):
                gpsimd.wait_ge(s_out, tt + 1)
                gpsimd.dma_start(out=out_h[128 * tt : 128 * (tt + 1), :],
                                 in_=outt(tt % 2)).then_inc(st_out2[tt % 2], 16)
            gpsimd.wait_ge(st_out2[0], 16 * 8)
            gpsimd.wait_ge(st_out2[1], 16 * 8)
        else:
            gpsimd.wait_ge(st_oacc2[0], 16 * 8)
            gpsimd.wait_ge(st_oacc2[1], 16 * 8)
        if debug:
            emit_dbg_taps(gpsimd)

    @blkc.tensor
    def _(tensor):
        ride = {"sem": None, "left": 0}

        def _r(inst):
            if ride["sem"] is not None:
                ride["left"] -= 1
                if ride["left"] == 0:
                    inst.then_inc(ride["sem"], 1)
                    ride["sem"] = None
            return inst

        def mm(*a, **k):
            return _r(tensor.matmul(*a, **k))

        def tp(*a, **k):
            return _r(tensor.transpose(*a, **k))

        def set_ride(sem_, slack=2):
            assert ride["sem"] is None, "unflushed ride"
            ride["sem"] = sem_
            ride["left"] = slack

        def drain_inc(sem_):
            flush_ride()
            tensor.drain().then_inc(sem_, 1)

        def flush_ride():
            if ride["sem"] is not None:
                s_ = ride["sem"]
                ride["sem"] = None
                tensor.drain().then_inc(s_, 1)

        def mk_dbl(stepi, k):
            def f():
                tensor.wait_ge(s_lad, EVT_LAD["init"] if stepi == 0
                               else EVT_LAD[f"dbl{stepi - 1}"])
                m = min(k, 256 - k)
                mm(pm_lad[0:DS, 0:m], powk[:, 0:DS], rt[:, 0:m], start=True, stop=True)
                mm(pm_lad[0:DS, 128 : 128 + m], powk[:, DS : 2 * DS], bv[:, 0:m],
                   start=True, stop=True)
                if 2 * k < 256:
                    mm(pm_lad[0:DS, 256 : 256 + DS], powk[:, DS : 2 * DS],
                       powk[:, 0:DS], start=True, stop=True)
                    mm(pm_lad[0:DS, 272 : 272 + DS], powk[:, 0:DS],
                       powk[:, DS : 2 * DS], start=True, stop=True)
                drain_inc(s_ptb)
            return f

        def st_growbvt():
            tensor.wait_ge(s_lad, EVT_LAD["dbl7"])
            tensor.wait_ge(s_small, SM6)
            mm(pm_lad[0:1, 0:256], bp_sb[:], rt[:], start=True, stop=True)
            drain_inc(s_ptb)
            tensor.transpose(pm_lad[0:128, 256 : 256 + DS], bv[:, 0:128], i16[:])
            drain_inc(s_ptb)

        def st_bmatJ():
            tensor.wait_ge(s_tbe, EVT_TBE["bmat"])
            tensor.wait_ge(s_sm2, SM7)
            mm(pm_lad[0:128, 272 : 272 + DS], j128[:], bmat_s[:], start=True, stop=True)
            drain_inc(s_ptb)

        def mk_round(name, gate, ds):
            def f():
                tensor.wait_ge(s_tbe, EVT_TBE[gate])
                if name == "pdA":
                    tensor.wait_ge(s_gex, 2)
                col = 0
                for d in ds:
                    a, b = PDPAIRS[d]
                    if d <= 8:
                        mm(pm_lad[0:DS, col : col + DS], ph(a), pl(b),
                           start=True, stop=True)
                        col += DS
                    mm(pm_lad[0:DS, col : col + DS], pl(a), ph(b),
                       start=True, stop=True)
                    col += DS
                drain_inc(s_ptb)
            return f

        def st_pdrev():
            tensor.wait_ge(s_tbe, EVT_TBE["pdDc"])
            for d in range(1, 15):
                mm(pm_lad[0:DS, DS * (d - 1) : DS * d], j16[:], ph(d),
                   start=True, stop=True)
            drain_inc(s_ptb)

        pe_dbl = [mk_dbl(i, 1 << i) for i in range(8)]

        tensor.wait_ge(s_small, SM6)
        tensor.transpose(pm_lad[0:DS, 0:DS], a_sb[:], i16[:])
        drain_inc(s_ptb)
        def emit_ut_psq(pb):
            if pb == 0:
                tensor.wait_ge(s_pdt, 1)
                tensor.wait_ge(s_tbe, EVT_TBE["qtbf1"])
            tensor.wait_ge(s_ucv, 16 * (pb + 1))
            for ci in range(5):
                if ci < 4:
                    c = 4 * pb + ci
                    if c >= 2:
                        tensor.wait_ge(s_useq, c - 1)
                    if ci == 1:
                        blk = pb + 1 if pb < NB - 1 else NB - 1
                        lastg = max(g_ for g_ in range(16 * blk, 16 * blk + 16)
                                    if g_ % 3 == 2)
                        tensor.wait_ge(s_usb, lastg + 1)
                    for s_i in range(NE):
                        if ci % 2 == 0:
                            dst = putA[:, 128 * s_i : 128 * (s_i + 1)]
                        else:
                            dst = putB[s_i // 8][:, 128 * (s_i % 8) : 128 * (s_i % 8 + 1)]
                        tp(dst, ucv(pb % 2, s_i)[:, 128 * ci : 128 * (ci + 1)], i128[:])
                    if ci == 0:
                        drain_inc(s_ptu)
                    else:
                        set_ride(s_ptu)
                else:
                    tensor.wait_ge(s_useq, 4 * pb + 4)
                if ci >= 1:
                    cm = ci - 1
                    tensor.wait_ge(s_useq, 4 * pb + cm + 1)
                    if cm == 0:
                        tensor.wait_ge(s_sall, 2 * pb)
                    for half in range(2):
                        for nb_ in range(2):
                            mm(psq2[half][:, 512 * nb_ : 512 * (nb_ + 1)],
                               bquad[:, 64 * cm : 64 * (cm + 1)],
                               useq(4 * pb + cm)[:, 1024 * half + 512 * nb_ :
                                                 1024 * half + 512 * (nb_ + 1)],
                               start=(cm == 0), stop=(cm == 3))
            drain_inc(s_pA)

        wg = 0
        for bb in range(NB):
            for ti in range(4):
                t = 4 * bb + ti
                tensor.wait_ge(s_ln, t + 1)
                if t >= 2:
                    tensor.wait_ge(s_xnT, t - 1)
                if ti == 0 and bb >= 2:
                    tensor.wait_ge(s_useq, 4 * (bb - 1))
                if ti % 2 == 1 and bb >= 1:
                    tensor.wait_ge(s_usb, 16 * bb)
                for kk in range(ND):
                    tp(pxt2[t % 2][:, 128 * kk : 128 * (kk + 1)],
                       xnb_t(t)[:, 128 * kk : 128 * (kk + 1)], i128[:])
                drain_inc(s_ptx)
                if bb == 0:
                    pe_dbl[2 * ti]()
                    pe_dbl[2 * ti + 1]()
                    if ti == 3:
                        st_growbvt()
            tensor.wait_ge(s_xnT, 4 * (bb + 1))
            if bb == 0:
                tensor.wait_ge(s_wdma, 16 * ND)
            pe_tbl = {1: st_bmatJ, 3: mk_round(*PDRND[0]), 5: mk_round(*PDRND[1]),
                      7: mk_round(*PDRND[2]), 9: mk_round(*PDRND[3]), 11: st_pdrev}
            for s_i in range(NE):
                g = wg
                if bb == 0 and s_i in pe_tbl:
                    flush_ride()
                    pe_tbl[s_i]()
                if bb == 1 and s_i in (9, 11, 13):
                    flush_ride()
                    if s_i == 9:
                        tensor.wait_ge(s_tbl, TBL16["gtrp"])
                        mm(pm_qt[0:128, 0:128], j128[:], gtrp[:], start=True, stop=True)
                        drain_inc(s_ptb)
                    else:
                        ti2 = (s_i - 11) // 2
                        tensor.wait_ge(s_tbl, TBL16[f"qtrp{ti2}"])
                        tensor.wait_ge(s_tbe, EVT_TBE["gtbf" if ti2 == 0 else "qtbf0"])
                        mm(pm_qt[:], j128[:], qtrp[:, 256 * ti2 : 256 * (ti2 + 1)],
                           start=True, stop=True)
                        drain_inc(s_ptb)
                if g >= 3:
                    tensor.wait_ge(s_usb, g - 2)
                for kk in range(ND):
                    mm(pw[g % 3][:],
                       winb(kk)[:, 128 * s_i : 128 * (s_i + 1)],
                       xnT_t[:, S * kk + BLK * bb : S * kk + BLK * (bb + 1)],
                       start=(kk == 0), stop=(kk == ND - 1))
                set_ride(s_pwin)
                wg += 1
            flush_ride()
            if bb >= 1:
                emit_ut_psq(bb - 1)
        emit_ut_psq(NB - 1)
        if phase_limit < 2:
            return
        tensor.wait_ge(s_sall, 2 * NB)
        tensor.wait_ge(s_tbe, EVT_TBE["qtbf1"])
        for otile in range(2):
            if otile == 1:
                tensor.wait_ge(s_hst, 1)
            for half, pb in ((0, pb0), (1, pb1)):
                for kt in range(2):
                    for nb_ in range(2):
                        mm(pb[:, 512 * nb_ : 512 * (nb_ + 1)],
                           qtbf[:, 256 * kt + 128 * otile : 256 * kt + 128 * (otile + 1)],
                           sall[:, E * kt + 1024 * half + 512 * nb_ : E * kt + 1024 * half + 512 * (nb_ + 1)],
                           start=(kt == 0), stop=(kt == 1))
            drain_inc(s_pB)
        if phase_limit < 3 and phase_limit != 25:
            return
        tensor.wait_ge(s_hst, 2)
        tensor.wait_ge(s_ct3, 48)
        nchc = NCH_C[0] if phase_limit == 25 else NCH
        for c in range(nchc + 1):
            if c < nchc:
                if c >= 1:
                    tensor.wait_ge(s_yt, c)
                tensor.wait_ge(s_hrl, 16 * (c + 1))
                t3, m3 = divmod(c, 3)
                parts = C25_PARTS[0] if phase_limit == 25 else 7
                for s_i in range(NE):
                    if parts & 1:
                        _dst = (po1[0] if s_i < 8 else po1[1]) if (parts & 32) else pc
                        _off = (128 * (s_i % 8)) if (parts & 32) else 128 * s_i
                        mm(_dst[:, _off : _off + 128],
                           i128[:] if (parts & 16) else useq(c)[:, 128 * s_i : 128 * (s_i + 1)],
                           i128[:] if (parts & 8) else gtbf[:],
                           start=True, stop=not (parts & 2))
                    if parts & 2:
                        mm(pc[:, 128 * s_i : 128 * (s_i + 1)],
                           hall3(t3)[32 * m3 : 32 * m3 + 32, 128 * s_i : 128 * (s_i + 1)],
                           ct32[32 * m3 : 32 * m3 + 32, :], start=not (parts & 1), stop=True)
                if c == 0 or phase_limit == 25 or c == nchc - 1:
                    drain_inc(s_pC)
                else:
                    set_ride(s_pC)
            if phase_limit == 25:
                continue
            if c >= 1:
                cc = c - 1
                tensor.wait_ge(s_yt, cc + 1)
                if cc == 0:
                    tensor.wait_ge(s_wdma, 16 * (ND + NE))
                if cc >= 2:
                    tensor.wait_ge(s_oat, cc - 1)
                for nb_ in range(2):
                    for s_i in range(NE):
                        mm(po1[cc % 2][:, 512 * nb_ : 512 * (nb_ + 1)],
                           ytc(cc)[:, 128 * s_i : 128 * (s_i + 1)],
                           woutb(s_i)[:, 512 * nb_ : 512 * (nb_ + 1)],
                           start=(s_i == 0), stop=(s_i == NE - 1))
                drain_inc(s_pO1)
        if phase_limit < 4 and phase_limit != 35:
            return
        tensor.wait_ge(s_tbl, 16 * 1)
        tensor.wait_ge(s_sm2, SM7)
        tensor.wait_ge(s_yt, NCH)
        tensor.wait_ge(s_oat, NCH)
        for q in range(4):
            tensor.wait_ge(s_wq[q], 16 * 16)
            for fb in range(NB):
                for fi in range(8):
                    hg = (q * NB + fb) * 8 + fi
                    if hg >= 4:
                        tensor.wait_ge(s_hT, hg - 3)
                    for kk in range(ND):
                        mm(phh[hg % 4][:],
                           w1q(q % 2, kk)[:, 128 * fi : 128 * (fi + 1)],
                           xnT_t[:, S * kk + BLK * fb : S * kk + BLK * (fb + 1)],
                           start=(kk == 0), stop=(kk == ND - 1))
                    set_ride(s_ph)
                flush_ride()
                if phase_limit == 35:
                    continue
                tensor.wait_ge(s_hT, (q * NB + fb + 1) * 8)
                for ti in range(4):
                    tt = 4 * fb + ti
                    og = (q * NB + fb) * 4 + ti
                    if og >= 2:
                        prev = og - 2
                        if prev < 48:
                            tensor.wait_ge(s_o2t, prev + 1)
                        else:
                            tensor.wait_ge(s_out, prev - 48 + 1)
                    for s_i in range(8):
                        mm(po2[og % 2][:, 0:512],
                           hTf((q * NB + fb) % 2, s_i)[:, 128 * ti : 128 * (ti + 1)],
                           w2q(q % 2, s_i)[:, 0:512], start=(s_i == 0),
                           stop=(q < 3 and s_i == 7))
                        mm(po2[og % 2][:, 512:1024],
                           hTf((q * NB + fb) % 2, s_i)[:, 128 * ti : 128 * (ti + 1)],
                           w2q(q % 2, s_i)[:, 512:1024], start=(s_i == 0),
                           stop=(q < 3 and s_i == 7))
                    if q == 3:
                        mm(po2[og % 2][:, 0:512], ones1[0:2, :], brow[0:2, 0:512],
                           start=False, stop=True)
                        mm(po2[og % 2][:, 512:1024], ones1[0:2, :], brow[0:2, 512:1024],
                           start=False, stop=True)
                    if og == 63:
                        drain_inc(s_pO2)
                    else:
                        set_ride(s_pO2)
        flush_ride()

    @blkc.scalar
    def _(scalar):
        def dinc(sem_):
            scalar.drain().then_inc(sem_, 1)


        def a_init():
            scalar.wait_ge(s_small, SM6)
            scalar.wait_ge(s_ptb, EVT_PTB["atT"])
            scalar.activation(at2[:, 0:DS], a_sb[:], AF.Tanh)
            scalar.activation(at2[:, DS : 2 * DS], pm_lad[0:DS, 0:DS], AF.Tanh)
            scalar.drain()
            scalar.copy(powk[:, 0:DS], at2[:, 0:DS])
            scalar.copy(powk[:, DS : 2 * DS], at2[:, DS : 2 * DS])
            scalar.copy(rt[:, 0:1], cp_sb[:])
            scalar.copy(bv[:, 0:1], bp_sb[:])
            dinc(s_lad)
        def ln_act(t):
            s = t % 4
            scalar.wait_ge(s_lnS, t + 1)
            scalar.activation(rstd[:, s : s + 1], invv[:, s : s + 1], AF.Sqrt)
            scalar.drain()
            scalar.activation(negp[:, s : s + 1], negm[:, s : s + 1], AF.Copy,
                              scale=rstd[:, s : s + 1])
            scalar.drain()
            if t >= 2:
                scalar.wait_ge(s_ptx, t - 1)
            scalar.activation(xnb_t(t), xview(t), AF.Identity,
                              scale=rstd[:, s : s + 1], bias=negp[:, s : s + 1])
            dinc(s_ln)

        def useq_evac(c):
            scalar.wait_ge(s_ptu, c + 1)
            if c % 2 == 0:
                scalar.activation(useq(c)[:], putA[:], AF.Copy)
            else:
                scalar.activation(useq(c)[:, 0:1024], putB[0][:], AF.Copy)
                scalar.activation(useq(c)[:, 1024:2048], putB[1][:], AF.Copy)
            dinc(s_useq)

        ln_act(0)
        ln_act(1)
        a_init()
        for bb in range(NB):
            for ti in range(4):
                t = 4 * bb + ti
                scalar.wait_ge(s_ptx, t + 1)
                dst = xnT_t[:].rearrange("p (k u) -> p k u", k=ND)[:, :, 128 * t : 128 * (t + 1)]
                scalar.activation(dst, pxt2[t % 2][:], AF.Copy)
                dinc(s_xnT)
                if ti < 2:
                    ln_act(4 * bb + 2 + ti)
            def a_head():
                scalar.wait_ge(s_lad, EVT_LAD["dbl7"])
                scalar.copy(pdL[:, 0:DS], powk[:, 0:DS])
                scalar.copy(pdH[:, 0:DS], powk[:, DS : 2 * DS])
                dinc(s_tbe)
                scalar.wait_ge(s_ptb, EVT_PTB["grow"])
                scalar.memzero(grow[:, 0:128])
                scalar.copy(grow[:, 128:256], pm_lad[0:1, 0:128])
                dinc(s_gex)
                scalar.copy(ctbf[:], rt[:, 1:129])
                dinc(s_gex)
                scalar.wait_ge(s_ptb, EVT_PTB["bvt"])
                scalar.copy(bmat_s[:], pm_lad[0:128, 256 : 256 + DS])
                dinc(s_tbe)

            def a_bquad():
                scalar.wait_ge(s_ptb, EVT_PTB["bmatJ"])
                scalar.memzero(bquad[:])
                scalar.drain()
                for q in range(4):
                    scalar.copy(bquad[:, 64 * q + 16 * q : 64 * q + 16 * q + 16],
                                pm_lad[0:128, 272 : 272 + DS])
                dinc(s_tbe)

            def mk_acp(name, gate, ds):
                def f():
                    scalar.wait_ge(s_ptb, EVT_PTB[name])
                    col = 0
                    for d in ds:
                        if d <= 8:
                            scalar.copy(pl(d), pm_lad[0:DS, col : col + DS])
                            col += DS
                        scalar.copy(ph(d), pm_lad[0:DS, col : col + DS])
                        col += DS
                    dinc(s_tbe)
                return f

            def a_arev():
                scalar.wait_ge(s_ptb, EVT_PTB["pdR"])
                scalar.copy(pdall[:], pm_lad[0:DS, 0 : 14 * DS])
                dinc(s_pdt)

            act_tbl = {0: a_head, 2: a_bquad, 4: mk_acp(*PDRND[0]),
                       6: mk_acp(*PDRND[1]), 8: mk_acp(*PDRND[2]),
                       10: mk_acp(*PDRND[3]), 12: a_arev}
            for s_i in range(NE):
                g = bb * NE + s_i
                if g == 0:
                    scalar.wait_ge(s_cbp, 1)
                scalar.wait_ge(s_pwin, g + 1)
                if g >= 2:
                    scalar.wait_ge(s_ucv, g - 1)
                ut = utile(g % 2)
                hoff_r = 48 * (bb % 2) + 3 * s_i
                hoff_w = 48 * ((bb + 1) % 2) + 3 * s_i
                if bb == 0:
                    scalar.memzero(ut[:, 0:3])
                else:
                    scalar.copy(ut[:, 0:3], uhalo[:, hoff_r : hoff_r + 3])
                scalar.copy(ut[:, 3 : 3 + BLK], pw[g % 3][:])
                scalar.copy(uhalo[:, hoff_w : hoff_w + 3], pw[g % 3][:, BLK - 3 : BLK])
                scalar.activation(acc3b(g % 2), pw[g % 3][:], AF.Identity,
                                  scale=cwc[:, 4 * s_i + 3 : 4 * s_i + 4],
                                  bias=cbp[:, s_i : s_i + 1])
                dinc(s_usb)
                if bb == 0 and TAIL_ON_ACT and s_i in act_tbl:
                    act_tbl[s_i]()
                if bb == 1 and s_i in (9, 11, 13):
                    if s_i == 9:
                        scalar.wait_ge(s_ptb, EVT_PTB["gt"])
                        scalar.copy(gtbf[:], pm_qt[0:128, 0:128])
                        dinc(s_tbe)
                    else:
                        ti2 = (s_i - 11) // 2
                        scalar.wait_ge(s_ptb, EVT_PTB[f"qt{ti2}"])
                        scalar.copy(qtbf[:, 256 * ti2 : 256 * (ti2 + 1)], pm_qt[:])
                        dinc(s_tbe)
            if bb < NB - 1:
                ln_act(4 * (bb + 1))
                ln_act(4 * (bb + 1) + 1)
            if bb >= 1:
                for ci in range(4):
                    useq_evac(4 * (bb - 1) + ci)
        for ci in range(4):
            useq_evac(4 * (NB - 1) + ci)
        if phase_limit < 2:
            return
        for otile in range(2):
            scalar.wait_ge(s_pB, otile + 1)
            scalar.copy(hstage(otile)[:, 0:1024], pb0[:])
            scalar.copy(hstage(otile)[:, 1024:2048], pb1[:])
            dinc(s_hst)
        if phase_limit < 3 and phase_limit != 25:
            return
        for c in range(NCH_C[0] if phase_limit == 25 else NCH):
            scalar.wait_ge(s_pC, c + 1)
            if phase_limit == 25 and not (C25_PARTS[0] & 4):
                scalar.copy(ytc(c)[:, 0:4], j128[:, 0:4])
            else:
                scalar.activation(ytc(c)[:], pc[:], AF.Silu)
            dinc(s_yt)
        if (phase_limit < 4 and phase_limit != 35) or phase_limit == 25:
            return
        scalar.wait_ge(s_sm2, SM7)
        for q in range(4):
            for fb in range(NB):
                for fi in range(8):
                    hg = (q * NB + fb) * 8 + fi
                    scalar.wait_ge(s_ph, hg + 1)
                    blkno = q * NB + fb
                    if blkno >= 2 and phase_limit != 35:
                        scalar.wait_ge(s_pO2, 4 * blkno - 4)
                    scalar.activation(hTf(blkno % 2, fi)[:], phh[hg % 4][:], AF.Relu,
                                      bias=b1c[:, 8 * q + fi : 8 * q + fi + 1]
                                      ).then_inc(s_hT, 1)
        scalar.drain().then_inc(s_hT, 1)

    @blkc.vector
    def _(vector):
        def dinc(sem_):
            vector.drain().then_inc(sem_, 1)

        vector.memset(zq[:], 0.0)
        vector.memset(ct32[:], 0.0)
        dinc(s_gex)

        def vdbl(stepi, k):
            m = min(k, 256 - k)
            vector.wait_ge(s_ptb, EVT_PTB[f"dbl{stepi}"])
            vector.tensor_copy(rt[:, k : k + m], pm_lad[0:DS, 0:m])
            vector.tensor_copy(bv[:, k : k + m], pm_lad[0:DS, 128 : 128 + m])
            if 2 * k < 256:
                vector.tensor_copy(powk[:, 0:DS], pm_lad[0:DS, 256 : 256 + DS])
                vector.tensor_copy(powk[:, DS : 2 * DS], pm_lad[0:DS, 272 : 272 + DS])
            dinc(s_lad)

        def v_tables_tail():
            vector.tensor_copy(pdL[:, 0:DS], powk[:, 0:DS])
            vector.tensor_copy(pdH[:, 0:DS], powk[:, DS : 2 * DS])
            dinc(s_tbe)
            vector.wait_ge(s_ptb, EVT_PTB["grow"])
            vector.memset(grow[:, 0:128], 0.0)
            vector.tensor_copy(grow[:, 128:256], pm_lad[0:1, 0:128])
            dinc(s_gex)
            vector.wait_ge(s_small, SM6)
            vector.tensor_add(grow[:, 128:129], grow[:, 128:129], dp_sb[:])
            vector.tensor_copy(ctbf[:], rt[:, 1:129])
            dinc(s_gex)
            vector.wait_ge(s_ptb, EVT_PTB["bvt"])
            vector.tensor_copy(bmat_s[:], pm_lad[0:128, 256 : 256 + DS])
            dinc(s_tbe)
            vector.wait_ge(s_ptb, EVT_PTB["bmatJ"])
            vector.memset(bquad[:], 0.0)
            vector.drain()
            for q in range(4):
                vector.tensor_copy(bquad[:, 64 * q + 16 * q : 64 * q + 16 * q + 16],
                                   pm_lad[0:128, 272 : 272 + DS])
            dinc(s_tbe)
            for name, gate, ds in PDRND:
                vector.wait_ge(s_ptb, EVT_PTB[name])
                col = 0
                for d in ds:
                    if d <= 8:
                        vector.tensor_copy(pl(d), pm_lad[0:DS, col : col + DS])
                        col += DS
                    vector.tensor_copy(ph(d), pm_lad[0:DS, col : col + DS])
                    col += DS
                dinc(s_tbe)
            vector.wait_ge(s_ptb, EVT_PTB["pdR"])
            vector.tensor_copy(pdall[:], pm_lad[0:DS, 0 : 14 * DS])
            dinc(s_pdt)
        def ln_stats(t):
            if t in (2, 3):
                vector.wait_ge(s_x23[t - 2], 16)
            else:
                vector.wait_ge(s_x2[t % 2], 16 if t < 2 else 16 * (1 + (t - 2) // 2))
            vector.bn_stats(stat6[:, 0:6], xview(t)[:, 0:512])
            vector.bn_stats(stat6[:, 6:12], xview(t)[:, 512:1024])
            vector.drain()
            vector.bn_aggr(mv[:, 2 * (t % 4) : 2 * (t % 4) + 2], stat6[:])
            vector.drain()
            vector.tensor_scalar(invv[:, t % 4 : t % 4 + 1],
                                 mv[:, 2 * (t % 4) + 1 : 2 * (t % 4) + 2],
                                 EPS, None, ALU.add)
            vector.drain()
            vector.reciprocal(invv[:, t % 4 : t % 4 + 1], invv[:, t % 4 : t % 4 + 1])
            vector.tensor_scalar(negm[:, t % 4 : t % 4 + 1],
                                 mv[:, 2 * (t % 4) : 2 * (t % 4) + 1],
                                 -1.0, None, ALU.mult)
            dinc(s_lnS)

        ln_stats(0)
        ln_stats(1)
        vdbl(0, 1)
        vdbl(1, 2)
        vector.wait_ge(s_sm2, SM7)
        for s_i in range(NE):
            vector.tensor_reduce(wsum[:, s_i : s_i + 1], cwc[:, 4 * s_i : 4 * (s_i + 1)],
                                 AX.X, ALU.add)
        vector.drain()
        for s_i in range(NE):
            vector.scalar_tensor_tensor(cbp[:, s_i : s_i + 1], binc[:, s_i : s_i + 1],
                                        wsum[:, s_i : s_i + 1], cbc[:, s_i : s_i + 1],
                                        ALU.mult, ALU.add)
        dinc(s_cbp)
        vdbl(2, 4)
        vdbl(3, 8)
        ln_stats(2)
        vdbl(4, 16)
        vdbl(5, 32)
        ln_stats(3)
        vdbl(6, 64)
        vdbl(7, 128)
        if TAIL_ON_ACT:
            vector.wait_ge(s_small, SM6)
            vector.wait_ge(s_gex, 2)
            vector.tensor_add(grow[:, 128:129], grow[:, 128:129], dp_sb[:])
            dinc(s_gex)
        else:
            v_tables_tail()
        for bb in range(NB):
            for s_i in range(NE):
                g = bb * NE + s_i
                vector.wait_ge(s_usb, g + 1)
                if s_i == 0 and bb >= 2:
                    vector.wait_ge(s_ptu, 4 * (bb - 1))
                ut = utile(g % 2)
                vector.scalar_tensor_tensor(wkb(0), ut[:, 2 : 2 + BLK],
                                            cwc[:, 4 * s_i + 2 : 4 * s_i + 3],
                                            acc3b(g % 2), ALU.mult, ALU.add)
                if rd_drains:
                    vector.drain()
                vector.scalar_tensor_tensor(wkb(1), ut[:, 1 : 1 + BLK],
                                            cwc[:, 4 * s_i + 1 : 4 * s_i + 2],
                                            wkb(0), ALU.mult, ALU.add)
                if rd_drains:
                    vector.drain()
                last = vector.scalar_tensor_tensor(ucv(bb % 2, s_i)[:], ut[:, 0 : 0 + BLK],
                                                   cwc[:, 4 * s_i + 0 : 4 * s_i + 1],
                                                   wkb(1), ALU.mult, ALU.add)
                if s_i == NE - 1:
                    dinc(s_ucv)
                else:
                    last.then_inc(s_ucv, 1)
                if s_i in (3, 7) and bb < NB - 1:
                    ln_stats(4 * (bb + 1) + s_i // 4)
            if bb < NB - 1:
                ln_stats(4 * (bb + 1) + 2)
                ln_stats(4 * (bb + 1) + 3)
            if bb >= 1:
                vector.wait_ge(s_pA, bb)
                for half in range(2):
                    vector.tensor_copy(
                        sall[64 * ((bb - 1) % 2) : 64 * ((bb - 1) % 2) + 64,
                             E * ((bb - 1) // 2) + 1024 * half :
                             E * ((bb - 1) // 2) + 1024 * (half + 1)],
                        psq2[half][:])
                    dinc(s_sall)
        if phase_limit >= 2:
            vector.wait_ge(s_ptu, 4 * (NB - 1))
            vector.memset(AR3[:, 4096:8192], 0.0)
            dinc(s_mz)
        vector.wait_ge(s_pA, NB)
        for half in range(2):
            vector.tensor_copy(
                sall[64 * ((NB - 1) % 2) : 64 * ((NB - 1) % 2) + 64,
                     E * ((NB - 1) // 2) + 1024 * half :
                     E * ((NB - 1) // 2) + 1024 * (half + 1)],
                psq2[half][:])
            dinc(s_sall)
        if phase_limit >= 2:
            vector.wait_ge(s_ptu, 4 * NB)
            vector.memset(AR3[:, 8192:16384], 0.0)
            dinc(s_mz)
        if phase_limit < 3 or phase_limit == 25:
            return
        for c in range(NCH):
            vector.wait_ge(s_pO1, c + 1)
            vector.wait_ge(s_xr2[c % 2], 16 * (c // 2 + 1))
            if c >= 2:
                vector.wait_ge(st_oacc2[c % 2], 16 * (c // 2))
            vector.tensor_add(oat(c % 2), po1[c % 2][:], xr(c % 2))
            dinc(s_oat)
        if phase_limit < 4 or phase_limit == 35:
            return
        for q in range(4):
            for tg in range(NT):
                og = q * NT + tg
                vector.wait_ge(s_pO2, og + 1)
                if og == 0:
                    vector.wait_ge(st_oacc2[0], 16 * 8)
                    vector.wait_ge(st_oacc2[1], 16 * 8)
                if og >= 2:
                    prev = og - 2
                    if prev < 48:
                        vector.wait_ge(st_add2[prev % 2], 16 * (prev // 2 + 1))
                    else:
                        o_ = prev - 48
                        vector.wait_ge(st_out2[o_ % 2], 16 * (o_ // 2 + 1))
                if q < 3:
                    vector.tensor_copy(outt(og % 2), po2[og % 2][:])
                    dinc(s_o2t)
                else:
                    vector.wait_ge(s_oar2[tg % 2], 16 * (tg // 2 + 1))
                    vector.tensor_add(outt(og % 2), po2[og % 2][:], oar(tg % 2))
                    dinc(s_out)

    ctx.close()
    return nc, dbg


def _host_constants():
    i128 = np.eye(128, dtype=ml_dtypes.bfloat16)
    j128 = np.eye(128, dtype=np.float32)[::-1].copy()
    j16 = np.eye(DS, dtype=np.float32)[::-1].copy()
    i16 = np.eye(DS, dtype=np.float32)
    ones1 = np.ones((2, 128), dtype=ml_dtypes.bfloat16)
    return i128, j128, j16, i16, ones1


def make_in_maps(inputs):
    x = np.asarray(inputs["L0_attn_x"], dtype=np.float32)
    Win = np.ascontiguousarray(np.asarray(inputs["L0_attn_W_in"], np.float32))
    b_in = np.asarray(inputs["L0_attn_b_in"], np.float32)
    conv_w = np.asarray(inputs["L0_attn_conv_w"], np.float32)
    conv_b = np.asarray(inputs["L0_attn_conv_b"], np.float32)
    A = np.ascontiguousarray(np.asarray(inputs["L0_attn_A"], np.float32))
    Bp = np.asarray(inputs["L0_attn_Bp"], np.float32).reshape(DS, 1)
    Cp = np.asarray(inputs["L0_attn_Cp"], np.float32).reshape(DS, 1)
    Dp = np.asarray(inputs["L0_attn_Dp"], np.float32).reshape(1, 1)
    Wout = np.ascontiguousarray(np.asarray(inputs["L0_attn_W_out"], np.float32))
    b_out = np.asarray(inputs["L0_attn_b_out"], np.float32)
    W1 = np.ascontiguousarray(np.asarray(inputs["L0_ffn_W1"], np.float32))
    b1 = np.asarray(inputs["L0_ffn_b1"], np.float32)
    W2 = np.ascontiguousarray(np.asarray(inputs["L0_ffn_W2"], np.float32))
    b2 = np.asarray(inputs["L0_ffn_b2"], np.float32)

    i128, j128, j16, i16, ones1 = _host_constants()
    binc = np.ascontiguousarray(b_in.reshape(NE, 128).T)
    cwc = np.ascontiguousarray(
        conv_w[:, 0, :].reshape(NE, 128, 4).transpose(1, 0, 2).reshape(128, NE * 4))
    cbc = np.ascontiguousarray(conv_b.reshape(NE, 128).T)
    b1c = np.ascontiguousarray(b1.reshape(NF, 128).T)
    brows = np.ascontiguousarray(np.stack([b_out, b2], axis=0))

    in_maps = []
    for core in range(8):
        in_maps.append({
            "x": np.ascontiguousarray(x[core]),
            "W_in": Win, "W_out": Wout, "W1": W1, "W2": W2,
            "A": A, "Bp": Bp, "Cp": Cp, "Dp": Dp,
            "b_in_cm": binc, "conv_w_cm": cwc, "conv_b_cm": cbc, "b1_cm": b1c,
            "brows": brows,
            "i128bf": i128, "j128f": j128, "j16f": j16, "i16f": i16,
            "ones2x128": ones1,
        })
    return in_maps


def kernel(**inputs):
    nc, _ = build_program(debug=False)
    in_maps = make_in_maps(inputs)
    res = run_bass_kernel_spmd(nc, in_maps, core_ids=list(range(8)))
    return np.stack([r["out"] for r in res.results], axis=0)


# version 21
# speedup vs baseline: 1.2128x; 1.2128x over previous
import contextlib

import numpy as np
import ml_dtypes

import concourse.bass as bass
import concourse.mybir as mybir
from concourse.bass_utils import run_bass_kernel_spmd

F32 = mybir.dt.float32
BF16 = mybir.dt.bfloat16
AF = mybir.ActivationFunctionType
ALU = mybir.AluOpType
AX = mybir.AxisListType

S, D, E, FF, DS = 2048, 1024, 2048, 4096, 16
NCH, NB, BLK, NT, ND, NE, NF = 16, 4, 512, 16, 8, 16, 32
EPS = 1e-5

SL_WIN0, SL_WOUT0, SL_FFN0 = 0, 16, 32
NSLICE = 96


def _ffn_slice(idx):
    j = idx - SL_FFN0
    q, r = divmod(j, 16)
    return (q, "w1", r) if r < 8 else (q, "w2", r - 8)


def _ffn_gate_pc(idx):
    q, kind, i = _ffn_slice(idx)
    if q >= 2:
        return None
    qp = q % 2
    col = 16384 * qp + (1024 * i if kind == "w1" else 8192 + 1024 * i)
    return col // 2048 + 1


def _mk_evt(names_):
    return {nm: i + 1 for i, nm in enumerate(names_)}


EVT_PTB = _mk_evt(
    ["atT"] + [f"dbl{i}" for i in range(8)] + ["grow", "bvt", "bmatJ"]
    + ["pdA", "pdB", "pdC", "pdD", "pdR", "gt", "qt0", "qt1"])
EVT_LAD = _mk_evt(["init"] + [f"dbl{i}" for i in range(8)])
EVT_TBE = _mk_evt(["pd1c", "bmat", "bquad", "pdAc", "pdBc", "pdCc", "pdDc",
                   "gtbf", "qtbf0", "qtbf1"])
EVT_TBL = _mk_evt(["brw", "qtfz", "gext", "gtrp"]
                  + [f"pdr{d}" for d in range(15)] + ["qtrp0", "qtrp1"])
TBL16 = {k: 16 * v for k, v in EVT_TBL.items()}

PDPAIRS = {2: (1, 1), 3: (1, 2), 4: (2, 2), 5: (1, 4), 6: (2, 4), 7: (3, 4),
           8: (4, 4), 9: (1, 8), 10: (2, 8), 11: (3, 8), 12: (4, 8),
           13: (5, 8), 14: (6, 8)}
PDRND = [("pdA", "pd1c", [2]), ("pdB", "pdAc", [3, 4]),
         ("pdC", "pdBc", [5, 6, 7, 8]), ("pdD", "pdCc", [9, 10, 11, 12, 13, 14])]


NCH_C = [NCH]
C25_PARTS = [7]


def build_program(debug=False, phase_limit=4, rd_drains=False, TAIL_ON_ACT=True):
    nc = bass.Bass("TRN2", target_bir_lowering=False)

    x_h = nc.dram_tensor("x", [S, D], F32, kind="ExternalInput").ap()
    Win_h = nc.dram_tensor("W_in", [D, E], F32, kind="ExternalInput").ap()
    Wout_h = nc.dram_tensor("W_out", [E, D], F32, kind="ExternalInput").ap()
    W1_h = nc.dram_tensor("W1", [D, FF], F32, kind="ExternalInput").ap()
    W2_h = nc.dram_tensor("W2", [FF, D], F32, kind="ExternalInput").ap()
    A_h = nc.dram_tensor("A", [DS, DS], F32, kind="ExternalInput").ap()
    Bp_h = nc.dram_tensor("Bp", [DS, 1], F32, kind="ExternalInput").ap()
    Cp_h = nc.dram_tensor("Cp", [DS, 1], F32, kind="ExternalInput").ap()
    Dp_h = nc.dram_tensor("Dp", [1, 1], F32, kind="ExternalInput").ap()
    binc_h = nc.dram_tensor("b_in_cm", [128, NE], F32, kind="ExternalInput").ap()
    cwc_h = nc.dram_tensor("conv_w_cm", [128, 4 * NE], F32, kind="ExternalInput").ap()
    cbc_h = nc.dram_tensor("conv_b_cm", [128, NE], F32, kind="ExternalInput").ap()
    b1c_h = nc.dram_tensor("b1_cm", [128, NF], F32, kind="ExternalInput").ap()
    brows_h = nc.dram_tensor("brows", [2, D], F32, kind="ExternalInput").ap()
    i128_h = nc.dram_tensor("i128bf", [128, 128], BF16, kind="ExternalInput").ap()
    j128_h = nc.dram_tensor("j128f", [128, 128], F32, kind="ExternalInput").ap()
    j16_h = nc.dram_tensor("j16f", [DS, DS], F32, kind="ExternalInput").ap()
    i16_h = nc.dram_tensor("i16f", [DS, DS], F32, kind="ExternalInput").ap()
    ones_h = nc.dram_tensor("ones2x128", [2, 128], BF16, kind="ExternalInput").ap()

    out_h = nc.dram_tensor("out", [S, D], F32, kind="ExternalOutput").ap()

    gext_h = nc.dram_tensor("g_ext", [1, 256], F32, kind="Internal").ap()
    qtf_h = nc.dram_tensor("qtF", [1, 8192], F32, kind="Internal").ap()
    oacc_h = nc.dram_tensor("oacc", [S, D], F32, kind="Internal").ap()

    dbg = {}
    DBG_TAPS = (("xnT", [128, ND * S], BF16), ("gtbf", [128, 128], BF16),
                ("ctbf", [DS, 128], BF16), ("ct32", [128, 128], BF16),
                ("qtbf", [128, 512], BF16), ("bquad", [128, 256], BF16),
                ("rt", [DS, 256], F32), ("bv", [DS, 256], F32),
                ("AR3", [128, 16384], BF16), ("sall", [128, 2 * E], BF16),
                ("AR2", [128, NCH * E], BF16),
                ("cbp", [128, NE], F32), ("brow", [2, D], BF16))
    if debug:
        for nm, shape, dt in DBG_TAPS:
            dbg[nm] = nc.dram_tensor("dbg_" + nm, shape, dt, kind="ExternalOutput").ap()
        dbg["oacc1"] = nc.dram_tensor("dbg_oacc1", [S, D], F32, kind="ExternalOutput").ap()

    ctx = contextlib.ExitStack()
    sbuf = lambda name, shape, dtype: ctx.enter_context(nc.sbuf_tensor(name, shape, dtype))
    sem = lambda name: ctx.enter_context(nc.semaphore(name))

    i128 = sbuf("i128", [128, 128], BF16)
    j128 = sbuf("j128", [128, 128], F32)
    j16 = sbuf("j16", [DS, DS], F32)
    i16 = sbuf("i16", [DS, DS], F32)
    ones1 = sbuf("ones1", [2, 128], BF16)
    a_sb = sbuf("a_sb", [DS, DS], F32)
    bp_sb = sbuf("bp_sb", [DS, 1], F32)
    cp_sb = sbuf("cp_sb", [DS, 1], F32)
    dp_sb = sbuf("dp_sb", [1, 1], F32)
    binc = sbuf("binc", [128, NE], F32)
    cwc = sbuf("cwc", [128, 4 * NE], F32)
    cbc = sbuf("cbc", [128, NE], F32)
    b1c = sbuf("b1c", [128, NF], F32)
    brow = sbuf("brow", [2, D], BF16)

    at2 = sbuf("at2", [DS, 2 * DS], F32)
    powk = sbuf("powk", [DS, 2 * DS], F32)
    pdL = sbuf("pdL", [DS, 8 * DS], F32)
    pdH = sbuf("pdH", [DS, 14 * DS], F32)
    pdall = sbuf("pdall", [DS, 14 * DS], F32)
    rt = sbuf("rt", [DS, 256], F32)
    bv = sbuf("bv", [DS, 256], F32)
    grow = sbuf("grow", [1, 256], F32)
    zq = sbuf("zq", [128, 64], F32)
    gtrp = sbuf("gtrp", [128, 128], F32)
    gtbf = sbuf("gtbf", [128, 128], BF16)
    ctbf = sbuf("ctbf", [DS, 128], BF16)
    ct32 = sbuf("ct32", [128, 128], BF16)
    bmat_s = sbuf("bmat_s", [128, DS], F32)
    bquad = sbuf("bquad", [128, 256], BF16)
    qtrp = sbuf("qtrp", [128, 512], F32)
    qtbf = sbuf("qtbf", [128, 512], BF16)
    cbp = sbuf("cbp", [128, NE], F32)
    wsum = sbuf("wsum", [128, NE], F32)

    stat6 = sbuf("stat6", [128, 12], F32)
    mv = sbuf("mv", [128, 8], F32)
    invv = sbuf("invv", [128, 4], F32)
    rstd = sbuf("rstd", [128, 4], F32)
    negm = sbuf("negm", [128, 4], F32)
    negp = sbuf("negp", [128, 4], F32)
    xnb = sbuf("xnb", [128, 2 * D], BF16)
    xnT_t = sbuf("xnT", [128, ND * S], BF16)
    uhalo = sbuf("uhalo", [128, 2 * 3 * NE], F32)
    sall = sbuf("sall", [128, 2 * E], BF16)

    AR1 = sbuf("AR1", [128, 16384], BF16)
    AR2 = sbuf("AR2", [128, NCH * E], BF16)
    AR3 = sbuf("AR3", [128, 16384], BF16)
    AR4 = sbuf("AR4", [128, 2 * D], F32)
    AR5 = sbuf("AR5", [128, 3078], F32)

    winb = lambda k: AR1[:, E * k : E * (k + 1)]
    woutb = lambda s: AR1[:, D * s : D * (s + 1)]
    useq = lambda c: AR2[:, E * c : E * (c + 1)]
    w1q = lambda qp, k: AR2[:, 16384 * qp + 1024 * k : 16384 * qp + 1024 * (k + 1)]
    w2q = lambda qp, f: AR2[:, 16384 * qp + 8192 + 1024 * f : 16384 * qp + 8192 + 1024 * (f + 1)]
    ucv = lambda par, s_: AR3[:, 8192 * par + BLK * s_ : 8192 * par + BLK * (s_ + 1)]
    xnb_t = lambda t: xnb[:, D * (t % 2) : D * (t % 2 + 1)]
    hstage = lambda t: AR3[:, E * t : E * (t + 1)]
    hall3 = lambda t: AR3[:, 4096 + E * t : 4096 + E * (t + 1)]
    hTf = lambda par, f: AR3[:, 4096 * par + BLK * f : 4096 * par + BLK * (f + 1)]
    xin = lambda p: AR4[:, D * p : D * (p + 1)]
    xr = xin
    oar = xin
    utile = lambda p: AR5[:, 515 * p : 515 * (p + 1)]
    acc3b = lambda p: AR5[:, 1030 + 512 * p : 1030 + 512 * (p + 1)]
    wkb = lambda p: AR5[:, 2054 + 512 * p : 2054 + 512 * (p + 1)]
    oat = lambda p: AR5[:, D * p : D * (p + 1)]
    outt = oat
    ytc = lambda c: sall[:, E * (c % 2) : E * (c % 2 + 1)]
    pl = lambda d: pdL[:, DS * (d - 1) : DS * d]
    ph = lambda d: pdH[:, DS * (d - 1) : DS * d]
    ar2f = AR2.bitcast(F32)
    xview = lambda t: ar2f[:, D * (t - 2) : D * (t - 1)] if t in (2, 3) else xin(t % 2)

    ps_arena_h = ctx.enter_context(nc.psum_tensor("ps_arena", [128, 4096], F32))
    ps_bc_h = ps_arena_h.bitcast(BF16)

    def bankf(b, n):
        return ps_arena_h[:, 512 * b : 512 * b + 512 * n]

    def bankb(b, n):
        return ps_bc_h[:, 1024 * b : 1024 * b + 1024 * n]

    pm_lad = bankf(0, 1)
    pm_qt = bankf(6, 1)[:, 0:256]
    pw = [bankf(3, 1), bankf(4, 1), bankf(2, 1)]
    pxt2 = [bankb(5, 1), bankb(2, 1)]
    putA = bankb(6, 2)
    putB = [bankb(5, 1), bankb(2, 1)]
    psq2 = [ps_arena_h[0:64, 0:1024], ps_arena_h[64:128, 0:1024]]
    pb0 = bankf(3, 2)
    pb1 = bankf(5, 2)
    pc = bankf(0, 4)
    po1 = [bankf(4, 2), bankf(6, 2)]
    phh = [bankf(i, 1) for i in range(4)]
    po2 = [bankf(4, 2), bankf(6, 2)]

    names_ = ("s_small s_wdma s_rsd s_ln s_lnS s_ptx s_xnT s_pwin s_usb s_ucv "
              "s_ptu s_useq s_pA s_sall s_tbl s_lad s_gex s_pdt s_ptb s_tbe s_pB "
              "s_hst s_hrl s_pC s_yt s_pO1 s_oat s_ph s_hT s_pO2 "
              "s_o2t s_out s_tap s_mz s_cbp s_ct3 s_sm2").split()
    SM = {n: sem(n) for n in names_}
    (s_small, s_wdma, s_rsd, s_ln, s_lnS, s_ptx, s_xnT, s_pwin, s_usb, s_ucv,
     s_ptu, s_useq, s_pA, s_sall, s_tbl, s_lad, s_gex, s_pdt, s_ptb, s_tbe, s_pB,
     s_hst, s_hrl, s_pC, s_yt, s_pO1, s_oat, s_ph, s_hT, s_pO2,
     s_o2t, s_out, s_tap, s_mz, s_cbp, s_ct3, s_sm2) = (SM[n] for n in names_)
    s_wq = [sem(f"s_wq{i}") for i in range(4)]
    s_x2 = [sem("s_x0"), sem("s_x1")]
    s_x23 = [sem("s_x23a"), sem("s_x23b")]
    s_xr2 = [sem("s_xr0"), sem("s_xr1")]
    s_oar2 = [sem("s_oar0"), sem("s_oar1")]
    st_oacc2 = [sem("st_oacc0"), sem("st_oacc1")]
    st_add2 = [sem("st_add0"), sem("st_add1")]
    st_out2 = [sem("st_out0"), sem("st_out1")]

    NSMALL = 13

    blkc = ctx.enter_context(nc.Block())

    _dbg_done = [0]

    def emit_dbg_taps(gpsimd):
        sbt = {"xnT": xnT_t, "gtbf": gtbf, "ctbf": ctbf, "ct32": ct32,
               "qtbf": qtbf, "bquad": bquad, "rt": rt, "bv": bv,
               "AR3": AR3, "sall": sall, "cbp": cbp, "brow": brow,
               "AR2": AR2}
        nd = 0
        for nm, shape, dt in DBG_TAPS:
            if nd:
                gpsimd.wait_ge(s_tap, 16 * nd)
            gpsimd.dma_start(out=dbg[nm], in_=sbt[nm][:]).then_inc(s_tap, 16)
            nd += 1
        if phase_limit in (3, 4):
            gpsimd.wait_ge(s_tap, 16 * nd)
            gpsimd.dma_start(out=dbg["oacc1"], in_=oacc_h).then_inc(s_tap, 16)
            nd += 1
        gpsimd.wait_ge(s_tap, 16 * nd)

    CONSTS = [(i128[:], i128_h), (a_sb[:], A_h), (i16[:], i16_h),
              (bp_sb[:], Bp_h), (cp_sb[:], Cp_h), (dp_sb[:], Dp_h),
              (binc[:], binc_h), (cwc[:], cwc_h), (cbc[:], cbc_h),
              (j16[:], j16_h), (j128[:], j128_h), (b1c[:], b1c_h),
              (ones1[:], ones_h)]
    SM6, SM7 = 16 * 6, 16 * 7

    @blkc.sync
    def _(sync):
        sync.dma_start(out=xin(0), in_=x_h[0:128, :]).then_inc(s_x2[0], 16)
        for dst, src in CONSTS[0:6]:
            sync.dma_start(out=dst, in_=src).then_inc(s_small, 16)
        for t in range(1, 4):
            sync.dma_start(out=xview(t), in_=x_h[128 * t : 128 * (t + 1), :]
                           ).then_inc(s_x2[1] if t == 1 else s_x23[t - 2], 16)
        for dst, src in CONSTS[6:]:
            sync.dma_start(out=dst, in_=src).then_inc(s_sm2, 16)
        for t in range(4, NT):
            sync.wait_ge(s_ln, t - 3 if t < 6 else t - 1)
            sync.dma_start(out=xin(t % 2), in_=x_h[128 * t : 128 * (t + 1), :]).then_inc(s_x2[t % 2], 16)
        sync.wait_ge(s_gex, 4 if TAIL_ON_ACT else 3)
        for q in range(3):
            sync.dma_start(out=ct32[32 * q : 32 * q + 16, :], in_=ctbf[:]
                           ).then_inc(s_ct3, 16)
        if phase_limit < 2:
            return
        for c in range(NCH):
            t3, m3 = divmod(c, 3)
            sync.wait_ge(s_mz, 1 if c < 6 else 2)
            sync.wait_ge(s_hst, c // 8 + 1)
            if c:
                sync.wait_ge(s_hrl, 16 * c)
            sync.dma_start(out=hall3(t3)[32 * m3 : 32 * m3 + 16, :],
                           in_=hstage(c // 8)[16 * (c % 8) : 16 * (c % 8) + 16, :]
                           ).then_inc(s_hrl, 16)
        if phase_limit < 3 or phase_limit == 25:
            return
        for c in range(NCH + 2):
            if c < NCH:
                if c >= 2:
                    sync.wait_ge(s_oat, c - 1)
                sync.dma_start(out=xr(c % 2),
                               in_=x_h[128 * c : 128 * (c + 1), :]).then_inc(s_xr2[c % 2], 16)
            if c >= 2:
                cc = c - 2
                sync.wait_ge(s_oat, cc + 1)
                sync.dma_start(out=oacc_h[128 * cc : 128 * (cc + 1), :],
                               in_=oat(cc % 2)).then_inc(st_oacc2[cc % 2], 16)
        if phase_limit < 4:
            return
        for tt in range(NT):
            sync.wait_ge(st_add2[tt % 2], 16 * (17 + tt // 2))
            if tt >= 2:
                sync.wait_ge(s_out, tt - 1)
            sync.dma_start(out=oar(tt % 2),
                           in_=oacc_h[128 * tt : 128 * (tt + 1), :]).then_inc(s_oar2[tt % 2], 16)

    @blkc.gpsimd
    def _(gpsimd):
        for k in range(ND):
            gpsimd.dma_start(out=winb(k), in_=Win_h[128 * k : 128 * (k + 1), :]
                             ).then_inc(s_wdma, 16)
        tblc = [0]

        def tdma(**kw):
            if tblc[0]:
                gpsimd.wait_ge(s_tbl, 16 * tblc[0])
            gpsimd.dma_start(**kw).then_inc(s_tbl, 16)
            tblc[0] += 1

        tdma(out=brow[:], in_=brows_h)
        gpsimd.wait_ge(s_gex, 1)
        tdma(out=bass.AP(tensor=qtf_h.tensor, offset=0, ap=[[64, 128], [1, 64]]),
             in_=zq[:])
        gpsimd.wait_ge(s_gex, 4 if TAIL_ON_ACT else 3)
        tdma(out=gext_h, in_=grow[:])
        tdma(out=gtrp[:],
             in_=bass.AP(tensor=gext_h.tensor, offset=1, ap=[[1, 128], [1, 128]]))
        gpsimd.wait_ge(s_sm2, SM7)
        gpsimd.wait_ge(s_tbl, 16 * tblc[0])
        gpsimd.wait_ge(s_pdt, 1)
        for d_ in range(15):
            src = j16[:] if d_ == 0 else pdall[:, DS * (d_ - 1) : DS * d_]
            gpsimd.dma_start(out=bass.AP(tensor=qtf_h.tensor, offset=(16 + d_) * 256,
                                         ap=[[16, 16], [1, 16]]),
                             in_=src).then_inc(s_tbl, 16)
            tblc[0] += 1
        for tile_i, off in ((0, 8 * 256), (1, 0)):
            src = bass.AP(tensor=qtf_h.tensor, offset=off,
                          ap=[[256, 8], [16, 16], [256, 16], [1, 16]])
            tdma(out=qtrp[:, 256 * tile_i : 256 * (tile_i + 1)], in_=src)
        if phase_limit < 3 or phase_limit == 25:
            gpsimd.wait_ge(s_sall, 2 * NB if phase_limit == 1 else 0)
            if phase_limit in (2, 25):
                gpsimd.wait_ge(s_hrl, 16 * NCH)
            if phase_limit == 25:
                gpsimd.wait_ge(s_yt, NCH_C[0])
            if debug:
                emit_dbg_taps(gpsimd)
            return
        gpsimd.wait_ge(s_pwin, NB * NE)
        for s_ in range(NE):
            gpsimd.dma_start(out=woutb(s_), in_=Wout_h[128 * s_ : 128 * (s_ + 1), :]
                             ).then_inc(s_wdma, 16)
        def q_loads(q):
            if q == 0:
                gpsimd.wait_ge(s_pO1, 1)
            for k in range(ND):
                gate = _ffn_gate_pc(SL_FFN0 + 16 * q + k)
                if gate is not None:
                    gpsimd.wait_ge(s_pC, gate)
                else:
                    gpsimd.wait_ge(s_ph, 32 * (q - 1))
                gpsimd.dma_start(out=w1q(q % 2, k),
                                 in_=W1_h[128 * k : 128 * (k + 1), 1024 * q : 1024 * (q + 1)]
                                 ).then_inc(s_wq[q], 16)
            for f in range(8):
                gate = _ffn_gate_pc(SL_FFN0 + 16 * q + 8 + f)
                if gate is not None:
                    gpsimd.wait_ge(s_pC, gate)
                elif phase_limit == 35:
                    gpsimd.wait_ge(s_ph, 32 * (q - 1))
                else:
                    gpsimd.wait_ge(s_pO2, 16 * (q - 1))
                gpsimd.dma_start(out=w2q(q % 2, f),
                                 in_=W2_h[128 * (8 * q + f) : 128 * (8 * q + f + 1), :]
                                 ).then_inc(s_wq[q], 16)

        def q_adds(qq):
            for tt in range(NT):
                qtt = 16 * qq + tt
                gpsimd.wait_ge(s_o2t, qtt + 1)
                gpsimd.wait_ge(st_oacc2[tt % 2], 16 * (tt // 2 + 1))
                gpsimd.dma_start(out=oacc_h[128 * tt : 128 * (tt + 1), :],
                                 in_=outt(qtt % 2), accum_op=ALU.add
                                 ).then_inc(st_add2[qtt % 2], 16)

        if phase_limit == 35:
            for q in range(4):
                q_loads(q)
            gpsimd.wait_ge(s_hT, 129)
        if phase_limit >= 4:
            q_loads(0)
            q_loads(1)
            q_adds(0)
            q_loads(2)
            q_adds(1)
            q_loads(3)
            q_adds(2)
        if phase_limit >= 4:
            for tt in range(NT):
                gpsimd.wait_ge(s_out, tt + 1)
                gpsimd.dma_start(out=out_h[128 * tt : 128 * (tt + 1), :],
                                 in_=outt(tt % 2)).then_inc(st_out2[tt % 2], 16)
            gpsimd.wait_ge(st_out2[0], 16 * 8)
            gpsimd.wait_ge(st_out2[1], 16 * 8)
        else:
            gpsimd.wait_ge(st_oacc2[0], 16 * 8)
            gpsimd.wait_ge(st_oacc2[1], 16 * 8)
        if debug:
            emit_dbg_taps(gpsimd)

    @blkc.tensor
    def _(tensor):
        ride = {"sem": None, "left": 0}

        def _r(inst):
            if ride["sem"] is not None:
                ride["left"] -= 1
                if ride["left"] == 0:
                    inst.then_inc(ride["sem"], 1)
                    ride["sem"] = None
            return inst

        def mm(*a, **k):
            return _r(tensor.matmul(*a, **k))

        def tp(*a, **k):
            return _r(tensor.transpose(*a, **k))

        def set_ride(sem_, slack=2):
            assert ride["sem"] is None, "unflushed ride"
            ride["sem"] = sem_
            ride["left"] = slack

        def drain_inc(sem_):
            flush_ride()
            tensor.drain().then_inc(sem_, 1)

        def flush_ride():
            if ride["sem"] is not None:
                s_ = ride["sem"]
                ride["sem"] = None
                tensor.drain().then_inc(s_, 1)

        def mk_dbl(stepi, k):
            def f():
                tensor.wait_ge(s_lad, EVT_LAD["init"] if stepi == 0
                               else EVT_LAD[f"dbl{stepi - 1}"])
                m = min(k, 256 - k)
                mm(pm_lad[0:DS, 0:m], powk[:, 0:DS], rt[:, 0:m], start=True, stop=True)
                mm(pm_lad[0:DS, 128 : 128 + m], powk[:, DS : 2 * DS], bv[:, 0:m],
                   start=True, stop=True)
                if 2 * k < 256:
                    mm(pm_lad[0:DS, 256 : 256 + DS], powk[:, DS : 2 * DS],
                       powk[:, 0:DS], start=True, stop=True)
                    mm(pm_lad[0:DS, 272 : 272 + DS], powk[:, 0:DS],
                       powk[:, DS : 2 * DS], start=True, stop=True)
                drain_inc(s_ptb)
            return f

        def st_growbvt():
            tensor.wait_ge(s_lad, EVT_LAD["dbl7"])
            tensor.wait_ge(s_small, SM6)
            mm(pm_lad[0:1, 0:256], bp_sb[:], rt[:], start=True, stop=True)
            drain_inc(s_ptb)
            tensor.transpose(pm_lad[0:128, 256 : 256 + DS], bv[:, 0:128], i16[:])
            drain_inc(s_ptb)

        def st_bmatJ():
            tensor.wait_ge(s_tbe, EVT_TBE["bmat"])
            tensor.wait_ge(s_sm2, SM7)
            mm(pm_lad[0:128, 272 : 272 + DS], j128[:], bmat_s[:], start=True, stop=True)
            drain_inc(s_ptb)

        def mk_round(name, gate, ds):
            def f():
                tensor.wait_ge(s_tbe, EVT_TBE[gate])
                if name == "pdA":
                    tensor.wait_ge(s_gex, 2)
                col = 0
                for d in ds:
                    a, b = PDPAIRS[d]
                    if d <= 8:
                        mm(pm_lad[0:DS, col : col + DS], ph(a), pl(b),
                           start=True, stop=True)
                        col += DS
                    mm(pm_lad[0:DS, col : col + DS], pl(a), ph(b),
                       start=True, stop=True)
                    col += DS
                drain_inc(s_ptb)
            return f

        def st_pdrev():
            tensor.wait_ge(s_tbe, EVT_TBE["pdDc"])
            for d in range(1, 15):
                mm(pm_lad[0:DS, DS * (d - 1) : DS * d], j16[:], ph(d),
                   start=True, stop=True)
            drain_inc(s_ptb)

        pe_dbl = [mk_dbl(i, 1 << i) for i in range(8)]

        tensor.wait_ge(s_small, SM6)
        tensor.transpose(pm_lad[0:DS, 0:DS], a_sb[:], i16[:])
        drain_inc(s_ptb)
        def emit_ut_psq(pb):
            if pb == 0:
                tensor.wait_ge(s_pdt, 1)
                tensor.wait_ge(s_tbe, EVT_TBE["qtbf1"])
            tensor.wait_ge(s_ucv, 16 * (pb + 1))
            for ci in range(5):
                if ci < 4:
                    c = 4 * pb + ci
                    if c >= 2:
                        tensor.wait_ge(s_useq, c - 1)
                    if ci == 1:
                        blk = pb + 1 if pb < NB - 1 else NB - 1
                        lastg = max(g_ for g_ in range(16 * blk, 16 * blk + 16)
                                    if g_ % 3 == 2)
                        tensor.wait_ge(s_usb, lastg + 1)
                    for s_i in range(NE):
                        if ci % 2 == 0:
                            dst = putA[:, 128 * s_i : 128 * (s_i + 1)]
                        else:
                            dst = putB[s_i // 8][:, 128 * (s_i % 8) : 128 * (s_i % 8 + 1)]
                        tp(dst, ucv(pb % 2, s_i)[:, 128 * ci : 128 * (ci + 1)], i128[:])
                    if ci == 0:
                        drain_inc(s_ptu)
                    else:
                        set_ride(s_ptu)
                else:
                    tensor.wait_ge(s_useq, 4 * pb + 4)
                if ci >= 1:
                    cm = ci - 1
                    tensor.wait_ge(s_useq, 4 * pb + cm + 1)
                    if cm == 0:
                        tensor.wait_ge(s_sall, 2 * pb)
                    for half in range(2):
                        for nb_ in range(2):
                            mm(psq2[half][:, 512 * nb_ : 512 * (nb_ + 1)],
                               bquad[:, 64 * cm : 64 * (cm + 1)],
                               useq(4 * pb + cm)[:, 1024 * half + 512 * nb_ :
                                                 1024 * half + 512 * (nb_ + 1)],
                               start=(cm == 0), stop=(cm == 3))
            drain_inc(s_pA)

        wg = 0
        for bb in range(NB):
            for ti in range(4):
                t = 4 * bb + ti
                tensor.wait_ge(s_ln, t + 1)
                if t >= 2:
                    tensor.wait_ge(s_xnT, t - 1)
                if ti == 0 and bb >= 2:
                    tensor.wait_ge(s_useq, 4 * (bb - 1))
                if ti % 2 == 1 and bb >= 1:
                    tensor.wait_ge(s_usb, 16 * bb)
                for kk in range(ND):
                    tp(pxt2[t % 2][:, 128 * kk : 128 * (kk + 1)],
                       xnb_t(t)[:, 128 * kk : 128 * (kk + 1)], i128[:])
                drain_inc(s_ptx)
                if bb == 0:
                    pe_dbl[2 * ti]()
                    pe_dbl[2 * ti + 1]()
                    if ti == 3:
                        st_growbvt()
            tensor.wait_ge(s_xnT, 4 * (bb + 1))
            if bb == 0:
                tensor.wait_ge(s_wdma, 16 * ND)
            pe_tbl = {1: st_bmatJ, 3: mk_round(*PDRND[0]), 5: mk_round(*PDRND[1]),
                      7: mk_round(*PDRND[2]), 9: mk_round(*PDRND[3]), 11: st_pdrev}
            for s_i in range(NE):
                g = wg
                if bb == 0 and s_i in pe_tbl:
                    flush_ride()
                    pe_tbl[s_i]()
                if bb == 1 and s_i in (9, 11, 13):
                    flush_ride()
                    if s_i == 9:
                        tensor.wait_ge(s_tbl, TBL16["gtrp"])
                        mm(pm_qt[0:128, 0:128], j128[:], gtrp[:], start=True, stop=True)
                        drain_inc(s_ptb)
                    else:
                        ti2 = (s_i - 11) // 2
                        tensor.wait_ge(s_tbl, TBL16[f"qtrp{ti2}"])
                        tensor.wait_ge(s_tbe, EVT_TBE["gtbf" if ti2 == 0 else "qtbf0"])
                        mm(pm_qt[:], j128[:], qtrp[:, 256 * ti2 : 256 * (ti2 + 1)],
                           start=True, stop=True)
                        drain_inc(s_ptb)
                if g >= 3:
                    tensor.wait_ge(s_usb, g - 2)
                for kk in range(ND):
                    mm(pw[g % 3][:],
                       winb(kk)[:, 128 * s_i : 128 * (s_i + 1)],
                       xnT_t[:, S * kk + BLK * bb : S * kk + BLK * (bb + 1)],
                       start=(kk == 0), stop=(kk == ND - 1))
                set_ride(s_pwin)
                wg += 1
            flush_ride()
            if bb >= 1:
                emit_ut_psq(bb - 1)
        emit_ut_psq(NB - 1)
        if phase_limit < 2:
            return
        tensor.wait_ge(s_sall, 2 * NB)
        tensor.wait_ge(s_tbe, EVT_TBE["qtbf1"])
        for otile in range(2):
            if otile == 1:
                tensor.wait_ge(s_hst, 1)
            for half, pb in ((0, pb0), (1, pb1)):
                for kt in range(2):
                    for nb_ in range(2):
                        mm(pb[:, 512 * nb_ : 512 * (nb_ + 1)],
                           qtbf[:, 256 * kt + 128 * otile : 256 * kt + 128 * (otile + 1)],
                           sall[:, E * kt + 1024 * half + 512 * nb_ : E * kt + 1024 * half + 512 * (nb_ + 1)],
                           start=(kt == 0), stop=(kt == 1))
            drain_inc(s_pB)
        if phase_limit < 3 and phase_limit != 25:
            return
        tensor.wait_ge(s_hst, 2)
        tensor.wait_ge(s_ct3, 48)
        nchc = NCH_C[0] if phase_limit == 25 else NCH
        for c in range(nchc + 1):
            if c < nchc:
                if c >= 1:
                    tensor.wait_ge(s_yt, c)
                tensor.wait_ge(s_hrl, 16 * (c + 1))
                t3, m3 = divmod(c, 3)
                parts = C25_PARTS[0] if phase_limit == 25 else 7
                for s_i in range(NE):
                    if parts & 1:
                        _dst = (po1[0] if s_i < 8 else po1[1]) if (parts & 32) else pc
                        _off = (128 * (s_i % 8)) if (parts & 32) else 128 * s_i
                        mm(_dst[:, _off : _off + 128],
                           i128[:] if (parts & 16) else useq(c)[:, 128 * s_i : 128 * (s_i + 1)],
                           i128[:] if (parts & 8) else gtbf[:],
                           start=True, stop=not (parts & 2))
                    if parts & 2:
                        mm(pc[:, 128 * s_i : 128 * (s_i + 1)],
                           hall3(t3)[32 * m3 : 32 * m3 + 32, 128 * s_i : 128 * (s_i + 1)],
                           ct32[32 * m3 : 32 * m3 + 32, :], start=not (parts & 1), stop=True)
                if c == 0 or phase_limit == 25 or c == nchc - 1:
                    drain_inc(s_pC)
                else:
                    set_ride(s_pC)
            if phase_limit == 25:
                continue
            if c >= 1:
                cc = c - 1
                tensor.wait_ge(s_yt, cc + 1)
                if cc == 0:
                    tensor.wait_ge(s_wdma, 16 * (ND + NE))
                if cc >= 2:
                    tensor.wait_ge(s_oat, cc - 1)
                for nb_ in range(2):
                    for s_i in range(NE):
                        mm(po1[cc % 2][:, 512 * nb_ : 512 * (nb_ + 1)],
                           ytc(cc)[:, 128 * s_i : 128 * (s_i + 1)],
                           woutb(s_i)[:, 512 * nb_ : 512 * (nb_ + 1)],
                           start=(s_i == 0), stop=(s_i == NE - 1))
                drain_inc(s_pO1)
        if phase_limit < 4 and phase_limit != 35:
            return
        tensor.wait_ge(s_tbl, 16 * 1)
        tensor.wait_ge(s_sm2, SM7)
        tensor.wait_ge(s_yt, NCH)
        tensor.wait_ge(s_oat, NCH)
        for q in range(4):
            tensor.wait_ge(s_wq[q], 16 * 16)
            for fb in range(NB):
                for fi in range(8):
                    hg = (q * NB + fb) * 8 + fi
                    if hg >= 4:
                        tensor.wait_ge(s_hT, hg - 3)
                    for kk in range(ND):
                        mm(phh[hg % 4][:],
                           w1q(q % 2, kk)[:, 128 * fi : 128 * (fi + 1)],
                           xnT_t[:, S * kk + BLK * fb : S * kk + BLK * (fb + 1)],
                           start=(kk == 0), stop=(kk == ND - 1))
                    set_ride(s_ph)
                flush_ride()
                if phase_limit == 35:
                    continue
                tensor.wait_ge(s_hT, (q * NB + fb + 1) * 8)
                for ti in range(4):
                    tt = 4 * fb + ti
                    og = (q * NB + fb) * 4 + ti
                    if og >= 2:
                        prev = og - 2
                        if prev < 48:
                            tensor.wait_ge(s_o2t, prev + 1)
                        else:
                            tensor.wait_ge(s_out, prev - 48 + 1)
                    for s_i in range(8):
                        mm(po2[og % 2][:, 0:512],
                           hTf((q * NB + fb) % 2, s_i)[:, 128 * ti : 128 * (ti + 1)],
                           w2q(q % 2, s_i)[:, 0:512], start=(s_i == 0),
                           stop=(q < 3 and s_i == 7))
                        mm(po2[og % 2][:, 512:1024],
                           hTf((q * NB + fb) % 2, s_i)[:, 128 * ti : 128 * (ti + 1)],
                           w2q(q % 2, s_i)[:, 512:1024], start=(s_i == 0),
                           stop=(q < 3 and s_i == 7))
                    if q == 3:
                        mm(po2[og % 2][:, 0:512], ones1[0:2, :], brow[0:2, 0:512],
                           start=False, stop=True)
                        mm(po2[og % 2][:, 512:1024], ones1[0:2, :], brow[0:2, 512:1024],
                           start=False, stop=True)
                    if og == 63:
                        drain_inc(s_pO2)
                    else:
                        set_ride(s_pO2)
        flush_ride()

    @blkc.scalar
    def _(scalar):
        def dinc(sem_):
            scalar.drain().then_inc(sem_, 1)


        def a_init():
            scalar.wait_ge(s_small, SM6)
            scalar.wait_ge(s_ptb, EVT_PTB["atT"])
            scalar.activation(at2[:, 0:DS], a_sb[:], AF.Tanh)
            scalar.activation(at2[:, DS : 2 * DS], pm_lad[0:DS, 0:DS], AF.Tanh)
            scalar.drain()
            scalar.copy(powk[:, 0:DS], at2[:, 0:DS])
            scalar.copy(powk[:, DS : 2 * DS], at2[:, DS : 2 * DS])
            scalar.copy(rt[:, 0:1], cp_sb[:])
            scalar.copy(bv[:, 0:1], bp_sb[:])
            dinc(s_lad)
        def ln_act(t):
            s = t % 4
            scalar.wait_ge(s_lnS, t + 1)
            scalar.activation(rstd[:, s : s + 1], invv[:, s : s + 1], AF.Sqrt)
            scalar.drain()
            scalar.activation(negp[:, s : s + 1], negm[:, s : s + 1], AF.Copy,
                              scale=rstd[:, s : s + 1])
            scalar.drain()
            if t >= 2:
                scalar.wait_ge(s_ptx, t - 1)
            scalar.activation(xnb_t(t), xview(t), AF.Identity,
                              scale=rstd[:, s : s + 1], bias=negp[:, s : s + 1])
            dinc(s_ln)

        def useq_evac(c):
            scalar.wait_ge(s_ptu, c + 1)
            if c % 2 == 0:
                scalar.activation(useq(c)[:], putA[:], AF.Copy)
            else:
                scalar.activation(useq(c)[:, 0:1024], putB[0][:], AF.Copy)
                scalar.activation(useq(c)[:, 1024:2048], putB[1][:], AF.Copy)
            dinc(s_useq)

        ln_act(0)
        ln_act(1)
        a_init()
        for bb in range(NB):
            for ti in range(4):
                t = 4 * bb + ti
                scalar.wait_ge(s_ptx, t + 1)
                dst = xnT_t[:].rearrange("p (k u) -> p k u", k=ND)[:, :, 128 * t : 128 * (t + 1)]
                scalar.activation(dst, pxt2[t % 2][:], AF.Copy)
                dinc(s_xnT)
                if ti < 2:
                    ln_act(4 * bb + 2 + ti)
            def a_head():
                scalar.wait_ge(s_lad, EVT_LAD["dbl7"])
                scalar.copy(pdL[:, 0:DS], powk[:, 0:DS])
                scalar.copy(pdH[:, 0:DS], powk[:, DS : 2 * DS])
                dinc(s_tbe)
                scalar.wait_ge(s_ptb, EVT_PTB["grow"])
                scalar.memzero(grow[:, 0:128])
                scalar.copy(grow[:, 128:256], pm_lad[0:1, 0:128])
                dinc(s_gex)
                scalar.copy(ctbf[:], rt[:, 1:129])
                dinc(s_gex)
                scalar.wait_ge(s_ptb, EVT_PTB["bvt"])
                scalar.copy(bmat_s[:], pm_lad[0:128, 256 : 256 + DS])
                dinc(s_tbe)

            def a_bquad():
                scalar.wait_ge(s_ptb, EVT_PTB["bmatJ"])
                scalar.memzero(bquad[:])
                scalar.drain()
                for q in range(4):
                    scalar.copy(bquad[:, 64 * q + 16 * q : 64 * q + 16 * q + 16],
                                pm_lad[0:128, 272 : 272 + DS])
                dinc(s_tbe)

            def mk_acp(name, gate, ds):
                def f():
                    scalar.wait_ge(s_ptb, EVT_PTB[name])
                    col = 0
                    for d in ds:
                        if d <= 8:
                            scalar.copy(pl(d), pm_lad[0:DS, col : col + DS])
                            col += DS
                        scalar.copy(ph(d), pm_lad[0:DS, col : col + DS])
                        col += DS
                    dinc(s_tbe)
                return f

            def a_arev():
                scalar.wait_ge(s_ptb, EVT_PTB["pdR"])
                scalar.copy(pdall[:], pm_lad[0:DS, 0 : 14 * DS])
                dinc(s_pdt)

            act_tbl = {0: a_head, 2: a_bquad, 4: mk_acp(*PDRND[0]),
                       6: mk_acp(*PDRND[1]), 8: mk_acp(*PDRND[2]),
                       10: mk_acp(*PDRND[3]), 12: a_arev}
            for s_i in range(NE):
                g = bb * NE + s_i
                if g == 0:
                    scalar.wait_ge(s_cbp, 1)
                scalar.wait_ge(s_pwin, g + 1)
                if g >= 2:
                    scalar.wait_ge(s_ucv, g - 1)
                ut = utile(g % 2)
                hoff_r = 48 * (bb % 2) + 3 * s_i
                hoff_w = 48 * ((bb + 1) % 2) + 3 * s_i
                if bb == 0:
                    scalar.memzero(ut[:, 0:3])
                else:
                    scalar.copy(ut[:, 0:3], uhalo[:, hoff_r : hoff_r + 3])
                scalar.copy(ut[:, 3 : 3 + BLK], pw[g % 3][:])
                scalar.copy(uhalo[:, hoff_w : hoff_w + 3], pw[g % 3][:, BLK - 3 : BLK])
                scalar.activation(acc3b(g % 2), pw[g % 3][:], AF.Identity,
                                  scale=cwc[:, 4 * s_i + 3 : 4 * s_i + 4],
                                  bias=cbp[:, s_i : s_i + 1])
                dinc(s_usb)
                if bb == 0 and TAIL_ON_ACT and s_i in act_tbl:
                    act_tbl[s_i]()
                if bb == 1 and s_i in (9, 11, 13):
                    if s_i == 9:
                        scalar.wait_ge(s_ptb, EVT_PTB["gt"])
                        scalar.copy(gtbf[:], pm_qt[0:128, 0:128])
                        dinc(s_tbe)
                    else:
                        ti2 = (s_i - 11) // 2
                        scalar.wait_ge(s_ptb, EVT_PTB[f"qt{ti2}"])
                        scalar.copy(qtbf[:, 256 * ti2 : 256 * (ti2 + 1)], pm_qt[:])
                        dinc(s_tbe)
            if bb < NB - 1:
                ln_act(4 * (bb + 1))
                ln_act(4 * (bb + 1) + 1)
            if bb >= 1:
                for ci in range(4):
                    useq_evac(4 * (bb - 1) + ci)
        for ci in range(4):
            useq_evac(4 * (NB - 1) + ci)
        if phase_limit < 2:
            return
        for otile in range(2):
            scalar.wait_ge(s_pB, otile + 1)
            scalar.copy(hstage(otile)[:, 0:1024], pb0[:])
            scalar.copy(hstage(otile)[:, 1024:2048], pb1[:])
            dinc(s_hst)
        if phase_limit < 3 and phase_limit != 25:
            return
        for c in range(NCH_C[0] if phase_limit == 25 else NCH):
            scalar.wait_ge(s_pC, c + 1)
            if phase_limit == 25 and not (C25_PARTS[0] & 4):
                scalar.copy(ytc(c)[:, 0:4], j128[:, 0:4])
            else:
                scalar.activation(ytc(c)[:], pc[:], AF.Silu)
            dinc(s_yt)
        if (phase_limit < 4 and phase_limit != 35) or phase_limit == 25:
            return
        scalar.wait_ge(s_sm2, SM7)
        for q in range(4):
            for fb in range(NB):
                for fi in range(8):
                    hg = (q * NB + fb) * 8 + fi
                    scalar.wait_ge(s_ph, hg + 1)
                    blkno = q * NB + fb
                    if blkno >= 2 and phase_limit != 35:
                        scalar.wait_ge(s_pO2, 4 * blkno - 4)
                    scalar.activation(hTf(blkno % 2, fi)[:], phh[hg % 4][:], AF.Relu,
                                      bias=b1c[:, 8 * q + fi : 8 * q + fi + 1]
                                      ).then_inc(s_hT, 1)
        scalar.drain().then_inc(s_hT, 1)

    @blkc.vector
    def _(vector):
        def dinc(sem_):
            vector.drain().then_inc(sem_, 1)

        vector.memset(zq[:], 0.0)
        vector.memset(ct32[:], 0.0)
        dinc(s_gex)

        def vdbl(stepi, k):
            m = min(k, 256 - k)
            vector.wait_ge(s_ptb, EVT_PTB[f"dbl{stepi}"])
            vector.tensor_copy(rt[:, k : k + m], pm_lad[0:DS, 0:m])
            vector.tensor_copy(bv[:, k : k + m], pm_lad[0:DS, 128 : 128 + m])
            if 2 * k < 256:
                vector.tensor_copy(powk[:, 0:DS], pm_lad[0:DS, 256 : 256 + DS])
                vector.tensor_copy(powk[:, DS : 2 * DS], pm_lad[0:DS, 272 : 272 + DS])
            dinc(s_lad)

        def v_tables_tail():
            vector.tensor_copy(pdL[:, 0:DS], powk[:, 0:DS])
            vector.tensor_copy(pdH[:, 0:DS], powk[:, DS : 2 * DS])
            dinc(s_tbe)
            vector.wait_ge(s_ptb, EVT_PTB["grow"])
            vector.memset(grow[:, 0:128], 0.0)
            vector.tensor_copy(grow[:, 128:256], pm_lad[0:1, 0:128])
            dinc(s_gex)
            vector.wait_ge(s_small, SM6)
            vector.tensor_add(grow[:, 128:129], grow[:, 128:129], dp_sb[:])
            vector.tensor_copy(ctbf[:], rt[:, 1:129])
            dinc(s_gex)
            vector.wait_ge(s_ptb, EVT_PTB["bvt"])
            vector.tensor_copy(bmat_s[:], pm_lad[0:128, 256 : 256 + DS])
            dinc(s_tbe)
            vector.wait_ge(s_ptb, EVT_PTB["bmatJ"])
            vector.memset(bquad[:], 0.0)
            vector.drain()
            for q in range(4):
                vector.tensor_copy(bquad[:, 64 * q + 16 * q : 64 * q + 16 * q + 16],
                                   pm_lad[0:128, 272 : 272 + DS])
            dinc(s_tbe)
            for name, gate, ds in PDRND:
                vector.wait_ge(s_ptb, EVT_PTB[name])
                col = 0
                for d in ds:
                    if d <= 8:
                        vector.tensor_copy(pl(d), pm_lad[0:DS, col : col + DS])
                        col += DS
                    vector.tensor_copy(ph(d), pm_lad[0:DS, col : col + DS])
                    col += DS
                dinc(s_tbe)
            vector.wait_ge(s_ptb, EVT_PTB["pdR"])
            vector.tensor_copy(pdall[:], pm_lad[0:DS, 0 : 14 * DS])
            dinc(s_pdt)
        def ln_stats(t):
            if t in (2, 3):
                vector.wait_ge(s_x23[t - 2], 16)
            else:
                vector.wait_ge(s_x2[t % 2], 16 if t < 2 else 16 * (1 + (t - 2) // 2))
            vector.bn_stats(stat6[:, 0:6], xview(t)[:, 0:512])
            vector.bn_stats(stat6[:, 6:12], xview(t)[:, 512:1024])
            vector.drain()
            vector.bn_aggr(mv[:, 2 * (t % 4) : 2 * (t % 4) + 2], stat6[:])
            vector.drain()
            vector.tensor_scalar(invv[:, t % 4 : t % 4 + 1],
                                 mv[:, 2 * (t % 4) + 1 : 2 * (t % 4) + 2],
                                 EPS, None, ALU.add)
            vector.drain()
            vector.reciprocal(invv[:, t % 4 : t % 4 + 1], invv[:, t % 4 : t % 4 + 1])
            vector.tensor_scalar(negm[:, t % 4 : t % 4 + 1],
                                 mv[:, 2 * (t % 4) : 2 * (t % 4) + 1],
                                 -1.0, None, ALU.mult)
            dinc(s_lnS)

        ln_stats(0)
        ln_stats(1)
        vdbl(0, 1)
        vdbl(1, 2)
        ln_stats(2)
        vdbl(2, 4)
        vdbl(3, 8)
        ln_stats(3)
        vdbl(4, 16)
        vdbl(5, 32)
        vdbl(6, 64)
        vdbl(7, 128)
        vector.wait_ge(s_sm2, SM7)
        for s_i in range(NE):
            vector.tensor_reduce(wsum[:, s_i : s_i + 1], cwc[:, 4 * s_i : 4 * (s_i + 1)],
                                 AX.X, ALU.add)
        vector.drain()
        for s_i in range(NE):
            vector.scalar_tensor_tensor(cbp[:, s_i : s_i + 1], binc[:, s_i : s_i + 1],
                                        wsum[:, s_i : s_i + 1], cbc[:, s_i : s_i + 1],
                                        ALU.mult, ALU.add)
        dinc(s_cbp)
        if TAIL_ON_ACT:
            vector.wait_ge(s_small, SM6)
            vector.wait_ge(s_gex, 2)
            vector.tensor_add(grow[:, 128:129], grow[:, 128:129], dp_sb[:])
            dinc(s_gex)
        else:
            v_tables_tail()
        for bb in range(NB):
            for s_i in range(NE):
                g = bb * NE + s_i
                vector.wait_ge(s_usb, g + 1)
                if s_i == 0 and bb >= 2:
                    vector.wait_ge(s_ptu, 4 * (bb - 1))
                ut = utile(g % 2)
                vector.scalar_tensor_tensor(wkb(0), ut[:, 2 : 2 + BLK],
                                            cwc[:, 4 * s_i + 2 : 4 * s_i + 3],
                                            acc3b(g % 2), ALU.mult, ALU.add)
                if rd_drains:
                    vector.drain()
                vector.scalar_tensor_tensor(wkb(1), ut[:, 1 : 1 + BLK],
                                            cwc[:, 4 * s_i + 1 : 4 * s_i + 2],
                                            wkb(0), ALU.mult, ALU.add)
                if rd_drains:
                    vector.drain()
                last = vector.scalar_tensor_tensor(ucv(bb % 2, s_i)[:], ut[:, 0 : 0 + BLK],
                                                   cwc[:, 4 * s_i + 0 : 4 * s_i + 1],
                                                   wkb(1), ALU.mult, ALU.add)
                if s_i == NE - 1:
                    dinc(s_ucv)
                else:
                    last.then_inc(s_ucv, 1)
                if s_i in (3, 7) and bb < NB - 1:
                    ln_stats(4 * (bb + 1) + s_i // 4)
            if bb < NB - 1:
                ln_stats(4 * (bb + 1) + 2)
                ln_stats(4 * (bb + 1) + 3)
            if bb >= 1:
                vector.wait_ge(s_pA, bb)
                for half in range(2):
                    vector.tensor_copy(
                        sall[64 * ((bb - 1) % 2) : 64 * ((bb - 1) % 2) + 64,
                             E * ((bb - 1) // 2) + 1024 * half :
                             E * ((bb - 1) // 2) + 1024 * (half + 1)],
                        psq2[half][:])
                    dinc(s_sall)
        if phase_limit >= 2:
            vector.wait_ge(s_ptu, 4 * (NB - 1))
            vector.memset(AR3[:, 4096:8192], 0.0)
            dinc(s_mz)
        vector.wait_ge(s_pA, NB)
        for half in range(2):
            vector.tensor_copy(
                sall[64 * ((NB - 1) % 2) : 64 * ((NB - 1) % 2) + 64,
                     E * ((NB - 1) // 2) + 1024 * half :
                     E * ((NB - 1) // 2) + 1024 * (half + 1)],
                psq2[half][:])
            dinc(s_sall)
        if phase_limit >= 2:
            vector.wait_ge(s_ptu, 4 * NB)
            vector.memset(AR3[:, 8192:16384], 0.0)
            dinc(s_mz)
        if phase_limit < 3 or phase_limit == 25:
            return
        for c in range(NCH):
            vector.wait_ge(s_pO1, c + 1)
            vector.wait_ge(s_xr2[c % 2], 16 * (c // 2 + 1))
            if c >= 2:
                vector.wait_ge(st_oacc2[c % 2], 16 * (c // 2))
            vector.tensor_add(oat(c % 2), po1[c % 2][:], xr(c % 2))
            dinc(s_oat)
        if phase_limit < 4 or phase_limit == 35:
            return
        for q in range(4):
            for tg in range(NT):
                og = q * NT + tg
                vector.wait_ge(s_pO2, og + 1)
                if og == 0:
                    vector.wait_ge(st_oacc2[0], 16 * 8)
                    vector.wait_ge(st_oacc2[1], 16 * 8)
                if og >= 2:
                    prev = og - 2
                    if prev < 48:
                        vector.wait_ge(st_add2[prev % 2], 16 * (prev // 2 + 1))
                    else:
                        o_ = prev - 48
                        vector.wait_ge(st_out2[o_ % 2], 16 * (o_ // 2 + 1))
                if q < 3:
                    vector.tensor_copy(outt(og % 2), po2[og % 2][:])
                    dinc(s_o2t)
                else:
                    vector.wait_ge(s_oar2[tg % 2], 16 * (tg // 2 + 1))
                    vector.tensor_add(outt(og % 2), po2[og % 2][:], oar(tg % 2))
                    dinc(s_out)

    ctx.close()
    return nc, dbg


def _host_constants():
    i128 = np.eye(128, dtype=ml_dtypes.bfloat16)
    j128 = np.eye(128, dtype=np.float32)[::-1].copy()
    j16 = np.eye(DS, dtype=np.float32)[::-1].copy()
    i16 = np.eye(DS, dtype=np.float32)
    ones1 = np.ones((2, 128), dtype=ml_dtypes.bfloat16)
    return i128, j128, j16, i16, ones1


def make_in_maps(inputs):
    x = np.asarray(inputs["L0_attn_x"], dtype=np.float32)
    Win = np.ascontiguousarray(np.asarray(inputs["L0_attn_W_in"], np.float32))
    b_in = np.asarray(inputs["L0_attn_b_in"], np.float32)
    conv_w = np.asarray(inputs["L0_attn_conv_w"], np.float32)
    conv_b = np.asarray(inputs["L0_attn_conv_b"], np.float32)
    A = np.ascontiguousarray(np.asarray(inputs["L0_attn_A"], np.float32))
    Bp = np.asarray(inputs["L0_attn_Bp"], np.float32).reshape(DS, 1)
    Cp = np.asarray(inputs["L0_attn_Cp"], np.float32).reshape(DS, 1)
    Dp = np.asarray(inputs["L0_attn_Dp"], np.float32).reshape(1, 1)
    Wout = np.ascontiguousarray(np.asarray(inputs["L0_attn_W_out"], np.float32))
    b_out = np.asarray(inputs["L0_attn_b_out"], np.float32)
    W1 = np.ascontiguousarray(np.asarray(inputs["L0_ffn_W1"], np.float32))
    b1 = np.asarray(inputs["L0_ffn_b1"], np.float32)
    W2 = np.ascontiguousarray(np.asarray(inputs["L0_ffn_W2"], np.float32))
    b2 = np.asarray(inputs["L0_ffn_b2"], np.float32)

    i128, j128, j16, i16, ones1 = _host_constants()
    binc = np.ascontiguousarray(b_in.reshape(NE, 128).T)
    cwc = np.ascontiguousarray(
        conv_w[:, 0, :].reshape(NE, 128, 4).transpose(1, 0, 2).reshape(128, NE * 4))
    cbc = np.ascontiguousarray(conv_b.reshape(NE, 128).T)
    b1c = np.ascontiguousarray(b1.reshape(NF, 128).T)
    brows = np.ascontiguousarray(np.stack([b_out, b2], axis=0))

    in_maps = []
    for core in range(8):
        in_maps.append({
            "x": np.ascontiguousarray(x[core]),
            "W_in": Win, "W_out": Wout, "W1": W1, "W2": W2,
            "A": A, "Bp": Bp, "Cp": Cp, "Dp": Dp,
            "b_in_cm": binc, "conv_w_cm": cwc, "conv_b_cm": cbc, "b1_cm": b1c,
            "brows": brows,
            "i128bf": i128, "j128f": j128, "j16f": j16, "i16f": i16,
            "ones2x128": ones1,
        })
    return in_maps


def kernel(**inputs):
    nc, _ = build_program(debug=False)
    in_maps = make_in_maps(inputs)
    res = run_bass_kernel_spmd(nc, in_maps, core_ids=list(range(8)))
    return np.stack([r["out"] for r in res.results], axis=0)
